# version 23
# baseline (speedup 1.0000x reference)
import numpy as np
from contextlib import ExitStack

import concourse.bass as bass
import concourse.mybir as mybir
from concourse.bass_utils import run_bass_kernel_spmd

F32 = mybir.dt.float32
BF16 = mybir.dt.bfloat16
AF = mybir.ActivationFunctionType
ALU = mybir.AluOpType
AX = mybir.AxisListType

NCORES = 8
S = 8192
D = 2048
QLR = 1536
KVLR = 512
H = 16
DN = 128
DR = 64
DV = 128
DQ = DN + DR
SCALE = float(DQ) ** -0.5
EPS_RMS = 1e-6
EPS_LN = 1e-5
NEGBIG = -1.0e30
MINF = -3.0e38

JT = 8
NT = 64
NG = 16
NGB = 2

ENGS = ("tensor", "vector", "scalar", "sync", "gpsimd")

DEBUG = set()
STOP_AFTER = None
P1_EVAC_V_ONLY = True
P1_TPS_LAG = True
P3_EV_SPLIT = True


class Prog:
    def __init__(self, cnt):
        self.ops = {e: [] for e in ENGS}
        self.cnt = cnt

    def add(self, eng, fn=None, waits=(), inc=None, inc_by=1):
        if inc is not None:
            self.cnt[inc] = self.cnt.get(inc, 0) + inc_by
        self.ops[eng].append((fn, tuple(waits), inc, inc_by))

    def val(self, sem):
        return self.cnt.get(sem, 0)


def _replay(prog, eng_obj, eng_name, sems):
    for fn, waits, inc, inc_by in prog.ops[eng_name]:
        for s, v in waits:
            if v > 0:
                eng_obj.wait_ge(sems[s], v)
        instr = fn(eng_obj) if fn is not None else None
        if inc is not None:
            assert instr is not None
            instr.then_inc(sems[inc], inc_by)


def _run_block(nc, name, prog, sems):
    with nc.Block(name, no_gpsimd_drain=True) as blk:
        for en in ENGS:
            getattr(blk, en)(lambda e, en=en: _replay(prog, e, en, sems))


SEM_NAMES = [
    "d_w", "d_x", "d_kout", "d_vout", "d_k", "d_v", "d_m", "d_out", "d_misc",
    "p1_mm", "p1_ev", "p1_sq", "p1_rc", "p1_rt", "p1_ap", "p1_tp", "p1_te",
    "p1_km", "p1_ke", "p1_vm", "p1_ve",
    "p2_mm", "p2_ev", "p2_sq", "p2_rc", "p2_rt", "p2_ap", "p2_tp", "p2_te",
    "q_mm", "q_ev", "q_ro", "q_tp", "q_te",
    "a_in", "t_s", "v_mx", "s_ex", "t_pt", "e_pt", "t_pv", "v_ou",
    "v_fi", "t_fi", "v_f2",
    "w_mm", "w_ev", "y_ln", "l_sm", "l_sq", "y_tp", "y_te",
    "f_mm", "f_ev", "z_ln", "cc", "e_pv", "d_xs",
    "d_xb", "d_kb", "d_vb", "d_m0", "d_m1", "d_m2", "d_m3", "d_voutb", "d_xsb",
    "ccd", "d_wb",
]


def build_program(debug=()):
    nc = bass.Bass("TRN2", target_bir_lowering=False)
    debug = set(debug)

    xtok_d = nc.dram_tensor("xtok", [JT, 128, D], F32, kind="ExternalInput").ap()
    xT_own_d = nc.dram_tensor("xT_own", [D, JT * 128], F32, kind="ExternalInput").ap()
    xT_kv_d = nc.dram_tensor("xT_kv", [D, JT * 128], F32, kind="ExternalInput").ap()
    wqaT_d = nc.dram_tensor("wqaT", [D, QLR], F32, kind="ExternalInput").ap()
    wqbT_d = nc.dram_tensor("wqbT", [QLR, H * DQ], F32, kind="ExternalInput").ap()
    wkvaT_d = nc.dram_tensor("wkvaT", [D, KVLR + DR], F32, kind="ExternalInput").ap()
    wkbkT_d = nc.dram_tensor("wkbkT", [KVLR, H * DN], F32, kind="ExternalInput").ap()
    wkbvT_d = nc.dram_tensor("wkbvT", [KVLR, H * DV], F32, kind="ExternalInput").ap()
    woT_d = nc.dram_tensor("woT", [H * DV, D], F32, kind="ExternalInput").ap()
    wfT_d = nc.dram_tensor("wfT", [D, D], F32, kind="ExternalInput").ap()
    gq_d = nc.dram_tensor("gq", [128, QLR], F32, kind="ExternalInput").ap()
    gkv_d = nc.dram_tensor("gkv", [128, KVLR], F32, kind="ExternalInput").ap()
    fs_d = nc.dram_tensor("fs", [128, D], F32, kind="ExternalInput").ap()
    fb_d = nc.dram_tensor("fb", [128, D], F32, kind="ExternalInput").ap()
    cosq_d = nc.dram_tensor("cosq", [JT, 128, H * DR], F32, kind="ExternalInput").ap()
    sinq_d = nc.dram_tensor("sinq", [JT, 128, H * DR], F32, kind="ExternalInput").ap()
    cosk_d = nc.dram_tensor("cosk", [JT, 128, DR], F32, kind="ExternalInput").ap()
    sink_d = nc.dram_tensor("sink", [JT, 128, DR], F32, kind="ExternalInput").ap()
    mask_d = nc.dram_tensor("mask", [JT, 128, 1024], F32, kind="ExternalInput").ap()
    ident_d = nc.dram_tensor("ident", [128, 128], F32, kind="ExternalInput").ap()
    out_d = nc.dram_tensor("out", [JT, 128, D], F32, kind="ExternalOutput").ap()

    kv_kind = "ExternalOutput" if "kv" in debug else "Internal"
    kcon = nc.dram_tensor("kcon", [H, DN, JT * 128], BF16, kind="Internal")
    vcon = nc.dram_tensor("vcon", [H, 128, JT * DV], BF16, kind="Internal")
    kpecon = nc.dram_tensor("kpecon", [DR, JT * 128], BF16, kind="Internal")
    kg = nc.dram_tensor("kg", [NCORES * H, DN, JT * 128], BF16, kind=kv_kind,
                        addr_space="Shared")
    vg = nc.dram_tensor("vg", [NCORES * H, 128, JT * DV], BF16, kind=kv_kind,
                        addr_space="Shared")
    kpeg = nc.dram_tensor("kpeg", [NCORES * DR, JT * 128], BF16, kind="Internal",
                          addr_space="Shared")
    kg_v = kg.ap().rearrange("(c h) p k -> c h p k", c=NCORES)
    vg_v = vg.ap().rearrange("(c h) p n -> c h p n", c=NCORES)
    kpeg_v = kpeg.ap().rearrange("(c d) k -> c d k", c=NCORES)
    dbg = {}
    if "qfm" in debug:
        dbg["dbg_q0"] = nc.dram_tensor("dbg_q0", [128, H, JT, 128], BF16, kind="ExternalOutput").ap()
        dbg["dbg_q1"] = nc.dram_tensor("dbg_q1", [DR, H, JT, 128], BF16, kind="ExternalOutput").ap()
    if "ofm" in debug:
        dbg["dbg_ofm"] = nc.dram_tensor("dbg_ofm", [128, H, JT, 128], BF16, kind="ExternalOutput").ap()
    if "yln" in debug:
        dbg["dbg_yln"] = nc.dram_tensor("dbg_yln", [128, JT, D], F32, kind="ExternalOutput").ap()
    if "kpe" in debug:
        dbg["dbg_kpe"] = nc.dram_tensor("dbg_kpe", [DR, S], BF16, kind="ExternalOutput").ap()

    cnt = {}
    with ExitStack() as top:
        sems = {n: top.enter_context(nc.semaphore(n)) for n in SEM_NAMES}
        ident_bf = top.enter_context(nc.sbuf_tensor("ident_bf", [128, 128], BF16))
        sc = top.enter_context(nc.sbuf_tensor("sc", [128, 8], F32))
        jk_v = top.enter_context(nc.sbuf_tensor("jk_v", [128, 2], F32))
        jk_s = top.enter_context(nc.sbuf_tensor("jk_s", [128, 2], F32))
        kpe_fm = top.enter_context(nc.sbuf_tensor("kpe_fm", [DR, S], BF16))
        ofm_ctx = ExitStack()

        def bubble_v(p, inc=None):
            p.add("vector", lambda e: e.tensor_copy(jk_v[:, 0:1], jk_v[:, 1:2]), inc=inc)

        def bubble_s(p, func, inc=None):
            p.add("scalar", lambda e, func=func: e.activation(
                out=jk_s[:, 0:1], in_=jk_s[:, 1:2], func=func), inc=inc)

        with ExitStack() as P1s:
            gkv_sb = P1s.enter_context(nc.sbuf_tensor("gkv_sb", [128, KVLR], F32))
            wkva_sb = P1s.enter_context(nc.sbuf_tensor("wkva_sb", [128, 16, KVLR + DR], BF16))
            wkbk_sb = P1s.enter_context(nc.sbuf_tensor("wkbk_sb", [128, 4, H * DN], BF16))
            wkbv_sb = P1s.enter_context(nc.sbuf_tensor("wkbv_sb", [128, 4, H * DV], BF16))
            cosk_sb = P1s.enter_context(nc.sbuf_tensor("cosk_sb", [128, JT, DR], F32))
            sink_sb = P1s.enter_context(nc.sbuf_tensor("sink_sb", [128, JT, DR], F32))
            kpe_own = P1s.enter_context(nc.sbuf_tensor("kpe_own", [DR, JT * 128], BF16))
            xg_sb = [P1s.enter_context(nc.sbuf_tensor(f"xg{i}", [128, 16, 512], BF16)) for i in range(2)]
            ckv_sb = [P1s.enter_context(nc.sbuf_tensor(f"ckv{i}", [128, KVLR], F32)) for i in range(2)]
            kperaw = [P1s.enter_context(nc.sbuf_tensor(f"kpr{i}", [128, DR], F32)) for i in range(2)]
            t1k = P1s.enter_context(nc.sbuf_tensor("t1k", [128, DR], F32))
            avk = P1s.enter_context(nc.sbuf_tensor("avk", [128, DR], F32))
            kpe_bf = [P1s.enter_context(nc.sbuf_tensor(f"kpb{i}", [128, DR], BF16)) for i in range(2)]
            ckvn_tm = [P1s.enter_context(nc.sbuf_tensor(f"cvn{i}", [128, KVLR], BF16)) for i in range(2)]
            ckvn_g = [P1s.enter_context(nc.sbuf_tensor(f"cvg{i}", [128, 4, 512], BF16)) for i in range(2)]
            kOut = [P1s.enter_context(nc.sbuf_tensor(f"kO{i}", [128, H, 512], BF16)) for i in range(2)]
            vOut = [P1s.enter_context(nc.sbuf_tensor(f"vO{i}", [128, H * DV], BF16)) for i in range(2)]
            dump1 = P1s.enter_context(nc.sbuf_tensor("dump1", [128, 512], BF16))

            ckv_ps = P1s.enter_context(nc.psum_tensor("ckv_ps", [128, KVLR + DR], F32))
            tp_ps = [P1s.enter_context(nc.psum_tensor(f"tp{i}", [128, 1024], BF16)) for i in range(2)]
            kx_ps = [P1s.enter_context(nc.psum_tensor(f"kx{i}", [128, 512], F32)) for i in range(2)]
            vx_ps = [P1s.enter_context(nc.psum_tensor(f"vx{i}", [128, 512], F32)) for i in range(2)]

            def c1(i):
                return sc[:, i:i + 1]

            p = Prog(cnt)
            p.add("gpsimd", lambda e: e.memset(jk_v[:], 0.0))
            p.add("gpsimd", lambda e: e.memset(jk_s[:], 0.0))
            p.add("gpsimd", lambda e: e.dma_start(out=ident_bf[:], in_=ident_d[:, :]),
                  inc="d_w", inc_by=16)
            p.add("gpsimd", lambda e: e.dma_start(
                out=wkva_sb[:], in_=wkvaT_d.rearrange("(c p) n -> p c n", p=128)),
                inc="d_w", inc_by=16)
            p.add("gpsimd", lambda e: e.dma_start(
                out=wkbk_sb[:], in_=wkbkT_d.rearrange("(c p) n -> p c n", p=128)),
                inc="d_w", inc_by=16)
            p.add("gpsimd", lambda e: e.dma_start(
                out=wkbv_sb[:], in_=wkbvT_d.rearrange("(c p) n -> p c n", p=128)),
                inc="d_w", inc_by=16)
            p.add("sync", lambda e: e.dma_start(
                out=cosk_sb[:], in_=cosk_d.rearrange("t p d -> p t d")), inc="d_misc", inc_by=16)
            p.add("sync", lambda e: e.dma_start(
                out=sink_sb[:], in_=sink_d.rearrange("t p d -> p t d")), inc="d_misc", inc_by=16)
            p.add("sync", lambda e: e.dma_start(out=gkv_sb[:], in_=gkv_d[:, :]),
                  inc="d_misc", inc_by=16)
            w_ready = p.val("d_w")
            gkv_ready = p.val("d_misc")

            def emit_xg(g):
                b = g % 2
                sem = "d_x" if b == 0 else "d_xb"
                waits = []
                if g >= 2:
                    waits.append(("p1_mm", 4 * (g - 1)))
                p.add("gpsimd", lambda e, g=g, b=b: e.dma_start(
                    out=xg_sb[b][:],
                    in_=xT_kv_d.rearrange("(c p) t -> p c t", p=128)[:, :, g * 512:(g + 1) * 512]),
                    waits=waits, inc=sem, inc_by=16)

            emit_xg(0)
            emit_xg(1)

            tps_done = -1

            def emit_tps_evac(t):
                pi = t % 2
                g2 = t // 4
                gb2 = g2 % 2
                tt2 = t % 4

                def tps(e, pi=pi):
                    last = None
                    for chn in range(4):
                        last = e.transpose(tp_ps[pi][:, chn * 128:(chn + 1) * 128],
                                           ckvn_tm[pi][:, chn * 128:(chn + 1) * 128],
                                           ident_bf[:])
                    return e.transpose(tp_ps[pi][0:DR, 512:640], kpe_bf[pi][:], ident_bf[:])
                tw = [("p1_ap", t + 1)]
                if t >= 2:
                    tw.append(("p1_te", t - 1))
                p.add("tensor", tps, waits=tw, inc="p1_tp")

                ew2 = [("p1_tp", t + 1)]
                if g2 >= 2 and tt2 == 0:
                    ew2 += [("p1_km", 16 * (g2 - 1)), ("p1_vm", 16 * (g2 - 1))]
                p.add("vector", lambda e, pi=pi, gb2=gb2, tt2=tt2: e.tensor_copy(
                    ckvn_g[gb2][:, :, tt2 * 128:(tt2 + 1) * 128],
                    tp_ps[pi][:, 0:512].rearrange("p (c x) -> p c x", x=128)),
                    waits=ew2)
                p.add("vector", lambda e, pi=pi, t=t: e.tensor_copy(
                    kpe_own[:, t * 128:(t + 1) * 128], tp_ps[pi][0:DR, 512:640]),
                    inc="p1_te")

            for g in range(NGB):
                if g + 2 < NGB:
                    emit_xg(g + 2)
                gb = g % 2
                for tt_ in range(4):
                    t = 4 * g + tt_
                    pi = t % 2

                    def ckv_mms(e, gb=gb, tt_=tt_):
                        last = None
                        for ch in range(16):
                            last = e.matmul(ckv_ps[:, 0:512],
                                            xg_sb[gb][:, ch, tt_ * 128:(tt_ + 1) * 128],
                                            wkva_sb[:, ch, 0:512],
                                            start=(ch == 0), stop=(ch == 15),
                                            skip_group_check=True)
                        for ch in range(16):
                            last = e.matmul(ckv_ps[:, 512:576],
                                            xg_sb[gb][:, ch, tt_ * 128:(tt_ + 1) * 128],
                                            wkva_sb[:, ch, 512:576],
                                            start=(ch == 0), stop=(ch == 15),
                                            skip_group_check=True)
                        return last
                    waits = [("d_x" if g % 2 == 0 else "d_xb", 16 * (g // 2 + 1)),
                             ("d_w", w_ready), ("d_misc", gkv_ready)]
                    if t >= 1:
                        waits.append(("p1_ev", t))
                    p.add("tensor", ckv_mms, waits=waits, inc="p1_mm")

                    ew = [("p1_mm", t + 1)]
                    if t >= 2:
                        ew.append(("p1_sq", t - 1))
                    p.add("vector", lambda e, pi=pi: e.tensor_copy(ckv_sb[pi][:], ckv_ps[:, 0:512]),
                          waits=ew)
                    p.add("vector", lambda e, pi=pi: e.tensor_copy(kperaw[pi][:], ckv_ps[:, 512:576]),
                          inc="p1_ev")
                    p.add("scalar", lambda e, pi=pi: e.activation(
                        out=dump1[:], in_=ckv_sb[pi][:], func=AF.Square,
                        accum_out=c1(0 + pi)),
                        waits=[("p1_ev", t + 1)], inc="p1_sq")
                    p.add("vector", lambda e, pi=pi: e.tensor_scalar(
                        c1(2 + pi), c1(0 + pi), 1.0 / KVLR, EPS_RMS, ALU.mult, ALU.add),
                        waits=[("p1_sq", t + 1)])
                    bubble_v(p)
                    p.add("vector", lambda e, pi=pi: e.reciprocal(c1(4 + pi), c1(2 + pi)))
                    bubble_v(p, inc="p1_rc")
                    p.add("scalar", lambda e, pi=pi: e.activation(
                        out=c1(6 + pi), in_=c1(4 + pi), func=AF.Sqrt),
                        waits=[("p1_rc", t + 1)])
                    bubble_s(p, AF.Sqrt, inc="p1_rt")
                    p.add("vector", lambda e, pi=pi: e.scalar_tensor_tensor(
                        ckvn_tm[pi][:], ckv_sb[pi][:], c1(6 + pi), gkv_sb[:],
                        ALU.mult, ALU.mult),
                        waits=[("p1_rt", t + 1), ("d_misc", gkv_ready)])
                    p.add("vector", lambda e, pi=pi: e.tensor_copy(t1k[:, 0:32], kperaw[pi][:, 32:64]))
                    p.add("vector", lambda e, pi=pi: e.tensor_copy(t1k[:, 32:64], kperaw[pi][:, 0:32]))
                    p.add("vector", lambda e, pi=pi, t=t: e.tensor_tensor(
                        avk[:], kperaw[pi][:], cosk_sb[:, t, :], ALU.mult))
                    p.add("vector", lambda e, t=t: e.tensor_tensor(
                        t1k[:], t1k[:], sink_sb[:, t, :], ALU.mult))
                    p.add("vector", lambda e, pi=pi: e.tensor_tensor(
                        kpe_bf[pi][:], avk[:], t1k[:], ALU.add), inc="p1_ap")

                    if P1_TPS_LAG:
                        if t >= 1 and tps_done < t - 1:
                            emit_tps_evac(t - 1)
                            tps_done = t - 1
                        if tt_ == 3:
                            emit_tps_evac(t)
                            tps_done = t
                    else:
                        emit_tps_evac(t)
                        tps_done = t

                for m in range(H):
                    mb = m % 2

                    def kmm(e, m=m, mb=mb, gb=gb):
                        last = None
                        for ch in range(4):
                            last = e.matmul(kx_ps[mb][:],
                                            wkbk_sb[:, ch, m * 128:(m + 1) * 128],
                                            ckvn_g[gb][:, ch, :],
                                            start=(ch == 0), stop=(ch == 3))
                        return last
                    kw = [("p1_te", 4 * (g + 1))]
                    if p.val("p1_ke") >= 2:
                        kw.append(("p1_ke", p.val("p1_ke") - 1))
                    p.add("tensor", kmm, waits=kw, inc="p1_km")
                    ew = [("p1_km", p.val("p1_km"))]
                    if g >= 2 and m == 0:
                        ew.append(("d_kout", 16 * (g - 1)))
                    if m % 2 == 0 or P1_EVAC_V_ONLY:
                        p.add("vector", lambda e, m=m, mb=mb, gb=gb: e.tensor_copy(
                            kOut[gb][:, m, :], kx_ps[mb][:]), waits=ew, inc="p1_ke")
                    else:
                        p.add("scalar", lambda e, m=m, mb=mb, gb=gb: e.copy(
                            out=kOut[gb][:, m, :], in_=kx_ps[mb][:]), waits=ew, inc="p1_ke")
                p.add("sync", lambda e, g=g, gb=gb: e.dma_start(
                    out=kcon.ap().rearrange("h p k -> p h k")[:, :, g * 512:(g + 1) * 512],
                    in_=kOut[gb][:]),
                    waits=[("p1_ke", p.val("p1_ke"))], inc="d_kout", inc_by=16)

                for tt_ in range(4):
                    t = 4 * g + tt_
                    vb = t % 2
                    for n in range(4):
                        nb = p.val("p1_vm") % 2

                        def vmm(e, tt_=tt_, n=n, nb=nb, gb=gb):
                            last = None
                            for ch in range(4):
                                last = e.matmul(vx_ps[nb][:],
                                                ckvn_g[gb][:, ch, tt_ * 128:(tt_ + 1) * 128],
                                                wkbv_sb[:, ch, n * 512:(n + 1) * 512],
                                                start=(ch == 0), stop=(ch == 3))
                            return last
                        vw = []
                        if p.val("p1_vm") >= 2:
                            vw.append(("p1_ve", p.val("p1_vm") - 1))
                        p.add("tensor", vmm, waits=vw, inc="p1_vm")
                        ew = [("p1_vm", p.val("p1_vm"))]
                        if t >= 2 and n == 0:
                            ew.append(("d_vout" if t % 2 == 0 else "d_voutb",
                                       p.val("d_vout" if t % 2 == 0 else "d_voutb")))
                        if n % 2 == 0 or P1_EVAC_V_ONLY:
                            p.add("vector", lambda e, vb=vb, n=n, nb=nb: e.tensor_copy(
                                vOut[vb][:, n * 512:(n + 1) * 512], vx_ps[nb][:]),
                                waits=ew, inc="p1_ve")
                        else:
                            p.add("scalar", lambda e, vb=vb, n=n, nb=nb: e.copy(
                                out=vOut[vb][:, n * 512:(n + 1) * 512], in_=vx_ps[nb][:]),
                                waits=ew, inc="p1_ve")
                    p.add("sync", lambda e, t=t, vb=vb: e.dma_start(
                        out=vcon.ap().rearrange("h p n -> p h n")[:, :,
                                                                  t * DV:(t + 1) * DV],
                        in_=vOut[vb][:].rearrange("p (h d) -> p h d", d=DV)),
                        waits=[("p1_ve", p.val("p1_ve"))],
                        inc="d_vout" if vb == 0 else "d_voutb", inc_by=16)

            p.add("sync", lambda e: e.dma_start(out=kpecon.ap(), in_=kpe_own[:]),
                  waits=[("p1_te", p.val("p1_te"))], inc="d_kout", inc_by=16)
            p.add("sync", waits=[("d_kout", p.val("d_kout")), ("d_vout", p.val("d_vout"))])
            p.add("gpsimd", waits=[("d_x", p.val("d_x")), ("d_xb", p.val("d_xb")),
                                   ("d_w", p.val("d_w")),
                                   ("d_kout", p.val("d_kout")),
                                   ("d_vout", p.val("d_vout")),
                                   ("d_voutb", p.val("d_voutb"))])
            if "kpe" in debug:
                p.add("sync", lambda e: e.dma_start(out=dbg["dbg_kpe"], in_=kpe_fm[:]),
                      waits=[("p1_te", p.val("p1_te"))], inc="d_misc", inc_by=16)
                p.add("sync", waits=[("d_misc", p.val("d_misc"))])
            _run_block(nc, "p1", p, sems)
            if STOP_AFTER == "p1":
                return nc

        o_fm = ofm_ctx.enter_context(nc.sbuf_tensor("o_fm", [128, H, JT, 128], BF16))
        qfm_ctx = ExitStack()
        q_fm0 = qfm_ctx.enter_context(nc.sbuf_tensor("q_fm0", [128, H, JT, 128], BF16))
        q_fm1 = qfm_ctx.enter_context(nc.sbuf_tensor("q_fm1", [DR, H, JT, 128], BF16))

        with ExitStack() as P2s:
            cqn_fm = P2s.enter_context(nc.sbuf_tensor("cqn_fm", [128, 12, JT * 128], BF16))
            cq_ps = [P2s.enter_context(nc.psum_tensor(f"cqp{i}", [128, 512], F32)) for i in range(3)]
            tp2_ps = [P2s.enter_context(nc.psum_tensor(f"tp2{i}", [128, 1024], BF16)) for i in range(2)]
            dump2 = P2s.enter_context(nc.sbuf_tensor("dump2", [128, 512], BF16))
            sc2 = P2s.enter_context(nc.sbuf_tensor("sc2", [128, 16], F32))

            def c2(i):
                return sc2[:, i:i + 1]

            p = Prog(cnt)

            with ExitStack() as P2a:
                gq_sb = P2a.enter_context(nc.sbuf_tensor("gq_sb", [128, QLR], F32))
                wqa_sb = P2a.enter_context(nc.sbuf_tensor("wqa_sb", [128, 16, QLR], BF16))
                xg2_0 = P2a.enter_context(nc.sbuf_tensor("xg2", [128, 16, 128], BF16))
                xg2 = [xg2_0, xg2_0]
                cq_sb = P2a.enter_context(nc.sbuf_tensor("cq_sb", [128, QLR], F32))
                cqn_tm0 = P2a.enter_context(nc.sbuf_tensor("cqn_tm", [128, QLR], BF16))
                cqn_tm = [cqn_tm0, cqn_tm0]

                p.add("gpsimd", lambda e: e.dma_start(
                    out=wqa_sb[:], in_=wqaT_d.rearrange("(c p) n -> p c n", p=128)),
                    inc="d_w", inc_by=16)
                p.add("sync", lambda e: e.dma_start(out=gq_sb[:], in_=gq_d[:, :]),
                      inc="d_misc", inc_by=16)
                wqa_ready = p.val("d_w")
                gq_ready = p.val("d_misc")
                x2base = p.val("d_x")

                def emit_x2(j):
                    waits = []
                    if j >= 1:
                        waits.append(("p2_mm", j))
                    p.add("gpsimd", lambda e, j=j: e.dma_start(
                        out=xg2[0][:],
                        in_=xT_own_d.rearrange("(c p) t -> p c t", p=128)[:, :, j * 128:(j + 1) * 128]),
                        waits=waits, inc="d_x", inc_by=16)

                emit_x2(0)
                for j in range(JT):
                    if j + 1 < JT:
                        emit_x2(j + 1)
                    pi = j % 2

                    def cq_mms(e, pi=pi):
                        last = None
                        for ch in range(16):
                            for n in range(3):
                                last = e.matmul(cq_ps[n][:],
                                                xg2[pi][:, ch, :],
                                                wqa_sb[:, ch, n * 512:(n + 1) * 512],
                                                start=(ch == 0), stop=(ch == 15),
                                                skip_group_check=True)
                        return last
                    waits = [("d_x", x2base + 16 * (j + 1)), ("d_w", wqa_ready)]
                    if j >= 1:
                        waits.append(("p2_ev", 3 * j))
                    p.add("tensor", cq_mms, waits=waits, inc="p2_mm")

                    for n in range(3):
                        ew = [("p2_mm", j + 1)]
                        if n == 0 and j >= 1:
                            ew.append(("p2_sq", j))
                        p.add("vector", lambda e, n=n: e.tensor_copy(
                            cq_sb[:, n * 512:(n + 1) * 512], cq_ps[n][:]),
                            waits=ew, inc="p2_ev")
                    sqw = [("p2_ev", 3 * (j + 1))]
                    if j >= 1:
                        sqw.append(("p2_tp", 3 * j))
                    p.add("scalar", lambda e, pi=pi: e.activation(
                        out=cqn_tm[pi][:], in_=cq_sb[:], func=AF.Square,
                        accum_out=c2(0 + pi)), waits=sqw, inc="p2_sq")
                    p.add("vector", lambda e, pi=pi: e.tensor_scalar(
                        c2(8 + pi), c2(0 + pi), 1.0 / QLR, EPS_RMS, ALU.mult, ALU.add),
                        waits=[("p2_sq", j + 1)])
                    bubble_v(p)
                    p.add("vector", lambda e, pi=pi: e.reciprocal(c2(10 + pi), c2(8 + pi)))
                    bubble_v(p, inc="p2_rc")
                    p.add("scalar", lambda e, pi=pi: e.activation(
                        out=c2(12 + pi), in_=c2(10 + pi), func=AF.Sqrt),
                        waits=[("p2_rc", j + 1)])
                    bubble_s(p, AF.Sqrt, inc="p2_rt")
                    aw = [("p2_rt", j + 1), ("d_misc", gq_ready)]
                    if j >= 1:
                        aw.append(("p2_tp", 3 * j))
                    for n in range(3):
                        p.add("vector", lambda e, pi=pi, n=n: e.scalar_tensor_tensor(
                            cqn_tm[pi][:, n * 512:(n + 1) * 512],
                            cq_sb[:, n * 512:(n + 1) * 512], c2(12 + pi),
                            gq_sb[:, n * 512:(n + 1) * 512], ALU.mult, ALU.mult),
                            waits=aw if n == 0 else (), inc="p2_ap" if n == 2 else None)
                    for r in range(3):
                        ridx = 3 * j + r
                        rb = ridx % 2

                        def tpr(e, pi=pi, r=r, rb=rb):
                            last = None
                            for q in range(4):
                                last = e.transpose(
                                    tp2_ps[rb][:, q * 128:(q + 1) * 128],
                                    cqn_tm[pi][:, (4 * r + q) * 128:(4 * r + q + 1) * 128],
                                    ident_bf[:])
                            return last
                        tw = [("p2_ap", j + 1)]
                        if ridx >= 2:
                            tw.append(("p2_te", ridx - 1))
                        p.add("tensor", tpr, waits=tw, inc="p2_tp")
                        p.add("vector", lambda e, r=r, rb=rb, j=j: e.tensor_copy(
                            cqn_fm[:, 4 * r:4 * r + 4, j * 128:(j + 1) * 128],
                            tp2_ps[rb][:, 0:512].rearrange("p (c x) -> p c x", x=128)),
                            waits=[("p2_tp", ridx + 1)], inc="p2_te")
                p.add("gpsimd", waits=[("d_w", p.val("d_w")), ("d_x", p.val("d_x"))])
                p.add("gpsimd", lambda e: e.collective_compute(
                    "AllGather", ALU.bypass, replica_groups=[list(range(NCORES))],
                    ins=[kcon.ap().opt()], outs=[kg.ap().opt()]), inc="cc")
                p.add("gpsimd", lambda e: e.collective_compute(
                    "AllGather", ALU.bypass, replica_groups=[list(range(NCORES))],
                    ins=[vcon.ap().opt()], outs=[vg.ap().opt()]), inc="cc")
                p.add("gpsimd", lambda e: e.collective_compute(
                    "AllGather", ALU.bypass, replica_groups=[list(range(NCORES))],
                    ins=[kpecon.ap().opt()], outs=[kpeg.ap().opt()]), inc="cc")
                p.add("gpsimd", lambda e: e.memset(jk_v[:], 0.0), inc="ccd")
                p.add("sync", waits=[("d_misc", p.val("d_misc"))])
                _run_block(nc, "p2a", p, sems)

            p = Prog(cnt)
            with ExitStack() as P2b:
                wqb_sb = P2b.enter_context(nc.sbuf_tensor("wqb_sb", [128, 12, 8 * DQ], BF16))
                q_tm = [P2b.enter_context(nc.sbuf_tensor(f"qtm{i}", [128, 8 * DQ], F32))
                        for i in range(2)]
                qbf = [P2b.enter_context(nc.sbuf_tensor(f"qbf{i}", [128, 8 * DQ], BF16))
                       for i in range(2)]
                t1q = P2b.enter_context(nc.sbuf_tensor("t1q", [128, 8 * DR], F32))
                avq = P2b.enter_context(nc.sbuf_tensor("avq", [128, 8 * DR], F32))
                cosq_sb = P2b.enter_context(nc.sbuf_tensor("cosq_sb", [128, 8 * DR * 2], F32))
                sinq_sb = P2b.enter_context(nc.sbuf_tensor("sinq_sb", [128, 8 * DR * 2], F32))

                for hh in range(2):
                    wqb_waits = [("q_mm", 8 * hh)]
                    if hh == 0:
                        wqb_waits += [("p2_mm", 8), ("p2_te", 24), ("p2_sq", 8)]
                    p.add("gpsimd", lambda e, hh=hh: e.dma_start(
                        out=wqb_sb[:],
                        in_=wqbT_d.rearrange("(c p) n -> p c n", p=128)[
                            :, :, hh * 8 * DQ:(hh + 1) * 8 * DQ]),
                        waits=wqb_waits, inc="d_w", inc_by=16)
                    wqb_ready = p.val("d_w")
                    for j in range(JT):
                        idx = 8 * hh + j
                        pi = idx % 2
                        rw = [("q_ro", idx - 1)] if idx >= 2 else [("p2_te", 24)]
                        p.add("sync", lambda e, j=j, hh=hh, pi=pi: e.dma_start(
                            out=cosq_sb[:, pi * 512:(pi + 1) * 512],
                            in_=cosq_d[j, :, hh * 512:(hh + 1) * 512]),
                            waits=rw, inc="d_misc", inc_by=16)
                        p.add("sync", lambda e, j=j, hh=hh, pi=pi: e.dma_start(
                            out=sinq_sb[:, pi * 512:(pi + 1) * 512],
                            in_=sinq_d[j, :, hh * 512:(hh + 1) * 512]),
                            inc="d_misc", inc_by=16)
                        rope_rdy = p.val("d_misc")

                        def qb_mms(e, j=j):
                            last = None
                            for ch in range(12):
                                for n in range(3):
                                    last = e.matmul(cq_ps[n][:],
                                                    cqn_fm[:, ch, j * 128:(j + 1) * 128],
                                                    wqb_sb[:, ch, n * 512:(n + 1) * 512],
                                                    start=(ch == 0), stop=(ch == 11),
                                                    skip_group_check=True)
                            return last
                        waits = [("d_w", wqb_ready), ("p2_te", 24)]
                        if idx >= 1:
                            waits.append(("q_ev", 3 * idx))
                        if idx == 0:
                            waits.append(("p2_ev", 24))
                        p.add("tensor", qb_mms, waits=waits, inc="q_mm")
                        for n in range(3):
                            ew = [("q_mm", idx + 1)]
                            if n == 0 and idx >= 2:
                                ew.append(("q_ro", idx - 1))
                            p.add("vector", lambda e, pi=pi, n=n: e.tensor_copy(
                                q_tm[pi][:, n * 512:(n + 1) * 512], cq_ps[n][:]),
                                waits=ew, inc="q_ev")

                        def pev(pi):
                            return q_tm[pi].ap().rearrange("p (h x) -> p h x", x=DQ)[:, :, DN:DQ]
                        t1v = t1q.ap().rearrange("p (h x) -> p h x", x=DR)
                        avv = avq.ap().rearrange("p (h x) -> p h x", x=DR)

                        def cosv(pi):
                            return cosq_sb[:, pi * 512:(pi + 1) * 512].rearrange(
                                "p (h x) -> p h x", x=DR)

                        def sinv(pi):
                            return sinq_sb[:, pi * 512:(pi + 1) * 512].rearrange(
                                "p (h x) -> p h x", x=DR)

                        p.add("vector", lambda e, pi=pi: e.tensor_copy(
                            t1v[:, :, 0:32], pev(pi)[:, :, 32:64]),
                            waits=[("q_ev", 3 * (idx + 1)), ("d_misc", rope_rdy)])
                        p.add("vector", lambda e, pi=pi: e.tensor_copy(
                            t1v[:, :, 32:64], pev(pi)[:, :, 0:32]))
                        p.add("vector", lambda e, pi=pi: e.tensor_tensor(
                            avv[:, :, :], pev(pi), cosv(pi), ALU.mult))
                        p.add("vector", lambda e, pi=pi: e.tensor_tensor(
                            t1v[:, :, :], t1v[:, :, :], sinv(pi), ALU.mult))
                        p.add("vector", lambda e, pi=pi: e.tensor_tensor(
                            pev(pi), avv[:, :, :], t1v[:, :, :], ALU.add))
                        cw = []
                        if idx >= 2:
                            cw.append(("q_tp", 8 * (idx - 1)))
                        p.add("vector", lambda e, pi=pi: e.tensor_copy(qbf[pi][:], q_tm[pi][:]),
                              waits=cw, inc="q_ro")
                        for h8 in range(8):
                            h = hh * 8 + h8
                            hb = h8 % 2
                            gidx = 8 * idx + h8

                            def qtp(e, pi=pi, h8=h8, hb=hb):
                                e.transpose(tp2_ps[hb][:, 0:128],
                                            qbf[pi][:, h8 * DQ:h8 * DQ + DN], ident_bf[:])
                                return e.transpose(tp2_ps[hb][0:DR, 128:256],
                                                   qbf[pi][:, h8 * DQ + DN:(h8 + 1) * DQ],
                                                   ident_bf[:])
                            tw = [("q_ro", idx + 1)]
                            if gidx >= 2:
                                tw.append(("q_te", gidx - 1))
                            p.add("tensor", qtp, waits=tw, inc="q_tp")
                            p.add("vector", lambda e, h=h, j=j, hb=hb: e.tensor_copy(
                                q_fm0[:, h, j, :], tp2_ps[hb][:, 0:128]),
                                waits=[("q_tp", gidx + 1)])
                            p.add("vector", lambda e, h=h, j=j, hb=hb: e.tensor_copy(
                                q_fm1[:, h, j, :], tp2_ps[hb][0:DR, 128:256]), inc="q_te")

                p.add("sync", waits=[("d_misc", p.val("d_misc"))])
                p.add("gpsimd", waits=[("d_w", p.val("d_w"))])
                if "qfm" in debug:
                    p.add("sync", lambda e: e.dma_start(out=dbg["dbg_q0"], in_=q_fm0[:]),
                          waits=[("q_te", p.val("q_te"))], inc="d_misc", inc_by=16)
                    p.add("sync", lambda e: e.dma_start(out=dbg["dbg_q1"], in_=q_fm1[:]),
                          inc="d_misc", inc_by=16)
                    p.add("sync", waits=[("d_misc", p.val("d_misc"))])
                _run_block(nc, "p2b", p, sems)

        if STOP_AFTER == "p2":
            qfm_ctx.close()
            ofm_ctx.close()
            return nc

        with ExitStack() as P3s:
            kbuf = [P3s.enter_context(nc.sbuf_tensor(f"kb{i}", [128, 8, 1024], BF16)) for i in range(2)]
            vbuf = [P3s.enter_context(nc.sbuf_tensor(f"vb{i}", [128, 8, 1024], BF16)) for i in range(2)]
            mask_sb = [P3s.enter_context(nc.sbuf_tensor(f"mk{i}", [128, 1024], BF16)) for i in range(4)]
            p_sb = [P3s.enter_context(nc.sbuf_tensor(f"p{i}", [128, 1024], BF16)) for i in range(2)]
            pt_sb = [P3s.enter_context(nc.sbuf_tensor(f"pt{i}", [128, 1024], BF16)) for i in range(2)]
            O_sb = P3s.enter_context(nc.sbuf_tensor("O_sb", [128, H, DN], F32))
            m_bufs = P3s.enter_context(nc.sbuf_tensor("m_bufs", [128, H, 2], F32))
            l_buf = P3s.enter_context(nc.sbuf_tensor("l_buf", [128, H], F32))
            c_buf = P3s.enter_context(nc.sbuf_tensor("c_buf", [128, H], F32))
            rs_buf = P3s.enter_context(nc.sbuf_tensor("rs_buf", [128, H], F32))
            nm_buf = P3s.enter_context(nc.sbuf_tensor("nm_buf", [128, H], F32))
            ofin = [P3s.enter_context(nc.sbuf_tensor(f"of{i}", [128, DN], BF16)) for i in range(2)]
            sc3 = P3s.enter_context(nc.sbuf_tensor("sc3", [128, 8], F32))

            s_ps = [P3s.enter_context(nc.psum_tensor(f"s{i}", [128, 1024], F32)) for i in range(2)]
            ptp_ps = [P3s.enter_context(nc.psum_tensor(f"ptp{i}", [128, 1024], BF16)) for i in range(2)]
            pv_ps = [P3s.enter_context(nc.psum_tensor(f"pv{i}", [128, 512], F32)) for i in range(2)]

            def c3(i):
                return sc3[:, i:i + 1]

            p = Prog(cnt)
            TILES = [(j, kt, hg) for j in range(JT) for kt in range(j + 1) for hg in range(2)]
            NTILE = len(TILES)
            NU = 8 * NTILE
            UNITS = []
            for i, (j, kt, hg) in enumerate(TILES):
                for h8 in range(8):
                    UNITS.append((i, j, kt, hg, hg * 8 + h8, h8))
            UEND = {}
            acc = 0
            for j in range(JT):
                acc += (j + 1) * 16
                UEND[j] = acc

            kbase = p.val("d_k")
            vbase = p.val("d_v")
            mbase = p.val("d_m")
            cc_need = p.val("cc")
            ccd_need = p.val("ccd")

            p.add("sync", lambda e: e.dma_start(
                out=kpe_fm[:].rearrange("d (c k) -> d c k", c=NCORES),
                in_=kpeg_v.rearrange("c d k -> d c k")),
                waits=[("cc", cc_need), ("ccd", ccd_need)], inc="d_misc", inc_by=16)
            kpe_rdy = p.val("d_misc")

            def emit_kv(i):
                j, kt, hg = TILES[i]
                b = i % 2
                ksem = "d_k" if b == 0 else "d_kb"
                vsem = "d_v" if b == 0 else "d_vb"
                kw, vw = [], []
                if i <= 1:
                    vw.append(("cc", cc_need))
                    vw.append(("ccd", ccd_need))
                    kw.append(("d_misc", kpe_rdy))
                if i >= 2:
                    kw.append(("t_s", 8 * (i - 1)))
                    vw.append(("t_pv", 8 * (i - 1)))
                p.add("sync", lambda e, kt=kt, hg=hg, b=b: e.dma_start(
                    out=kbuf[b][:],
                    in_=kg_v[kt, hg * 8:(hg + 1) * 8, :, :].rearrange("h p k -> p h k")),
                    waits=kw, inc=ksem, inc_by=16)
                p.add("gpsimd", lambda e, kt=kt, hg=hg, b=b: e.dma_start(
                    out=vbuf[b][:],
                    in_=vg_v[kt, hg * 8:(hg + 1) * 8, :, :].rearrange("h p n -> p h n")),
                    waits=vw, inc=vsem, inc_by=16)

            def emit_mask(j):
                w = []
                if j >= 4:
                    w.append(("v_mx", UEND[j - 4]))
                p.add("gpsimd", lambda e, j=j: e.dma_start(
                    out=mask_sb[j % 4][:], in_=mask_d[j, :, :]),
                    waits=w, inc="d_m%d" % (j % 4), inc_by=16)

            for jj in range(min(4, JT)):
                emit_mask(jj)
            emit_kv(0)
            emit_kv(1)

            def emit_A(u):
                i, j, kt, hg, h, h8 = UNITS[u]
                pi = u % 2
                b = i % 2

                def qk(e, pi=pi, h=h, h8=h8, b=b, kt=kt, j=j):
                    e.matmul(s_ps[pi][:, 0:512], q_fm0[:, h, j, :],
                             kbuf[b][:, h8, 0:512], start=True, stop=False,
                             skip_group_check=True)
                    e.matmul(s_ps[pi][:, 512:1024], q_fm0[:, h, j, :],
                             kbuf[b][:, h8, 512:1024], start=True, stop=False,
                             skip_group_check=True)
                    e.matmul(s_ps[pi][:, 0:512], q_fm1[:, h, j, :],
                             kpe_fm[:, kt * 1024:kt * 1024 + 512], start=False, stop=True,
                             skip_group_check=True)
                    return e.matmul(s_ps[pi][:, 512:1024], q_fm1[:, h, j, :],
                                    kpe_fm[:, kt * 1024 + 512:(kt + 1) * 1024],
                                    start=False, stop=True, skip_group_check=True)
                qw = [("d_k" if i % 2 == 0 else "d_kb", 16 * (i // 2 + 1))]
                if u >= 2:
                    qw.append(("s_ex", u - 1))
                p.add("tensor", qk, waits=qw, inc="t_s")

            def emit_G1(u):
                i, j, kt, hg, h, h8 = UNITS[u]
                p.add("vector", lambda e, h=h: e.scalar_tensor_tensor(
                    l_buf[:, h:h + 1], l_buf[:, h:h + 1], c_buf[:, h:h + 1],
                    rs_buf[:, h:h + 1], ALU.mult, ALU.add),
                    waits=[("s_ex", u + 1), ("t_pv", u + 1)])

            def emit_G2(u):
                i, j, kt, hg, h, h8 = UNITS[u]
                pi = u % 2
                p.add("vector", lambda e, h=h, pi=pi: e.scalar_tensor_tensor(
                    O_sb[:, h, :], O_sb[:, h, :], c_buf[:, h:h + 1],
                    pv_ps[pi][:, 0:DN], ALU.mult, ALU.add), inc="v_ou")

            def emit_D(uu):
                i2, j2, kt2, hg2, h2, h82 = UNITS[uu]
                pi2 = uu % 2

                def tps(e, pi=pi2):
                    last = None
                    for q in range(8):
                        last = e.transpose(ptp_ps[pi][:, q * 128:(q + 1) * 128],
                                           p_sb[pi][:, q * 128:(q + 1) * 128], ident_bf[:])
                    return last
                tw = [("s_ex", uu + 1)]
                if uu >= 2:
                    x = uu - 2
                    nv = (x + 1) // 2
                    if P3_EV_SPLIT:
                        if x % 2 == 1:
                            tw.append(("e_pv", nv))
                        else:
                            tw.append(("e_pt", x + 1 - nv))
                    else:
                        tw.append(("e_pt", uu - 1))
                if kt2 == 0 and hg2 == 0 and h82 == 0 and j2 >= 1:
                    tw.append(("v_f2", 16 * j2))
                p.add("tensor", tps, waits=tw, inc="t_pt")

            def emit_EF(uu):
                i2, j2, kt2, hg2, h2, h82 = UNITS[uu]
                pi2 = uu % 2
                if P3_EV_SPLIT and uu % 2 == 1:
                    p.add("vector", lambda e, pi=pi2: e.tensor_copy(
                        pt_sb[pi][:], ptp_ps[pi][:]),
                        waits=[("t_pt", uu + 1)], inc="e_pv")
                else:
                    p.add("scalar", lambda e, pi=pi2: e.copy(
                        out=pt_sb[pi][:], in_=ptp_ps[pi][:]),
                        waits=[("t_pt", uu + 1)], inc="e_pt")

                def pv(e, pi=pi2, h8=h82, b=i2 % 2):
                    last = None
                    for q in range(8):
                        last = e.matmul(pv_ps[pi][:, 0:DN],
                                        pt_sb[pi][:, q * 128:(q + 1) * 128],
                                        vbuf[b][:, h8, q * 128:(q + 1) * 128],
                                        start=(q == 0), stop=(q == 7),
                                        skip_group_check=True)
                    return last
                def _nv(x):
                    return (x + 1) // 2
                if P3_EV_SPLIT and uu % 2 == 1:
                    pw = [("e_pv", _nv(uu))]
                else:
                    pw = [("e_pt", (uu + 1 - _nv(uu)) if P3_EV_SPLIT else uu + 1)]
                pw.append(("d_v" if i2 % 2 == 0 else "d_vb", 16 * (i2 // 2 + 1)))
                if uu >= 2:
                    pw.append(("v_ou", uu - 1))
                p.add("tensor", pv, waits=pw, inc="t_pv")

            emit_A(0)
            u = 0
            g_done = -1
            d_done = -1
            ef_done = -1
            ev_done = -1

            def emit_Ev(uu):
                p.add("vector", lambda e, pi=uu % 2: e.tensor_copy(
                    pt_sb[pi][:, 0:384], ptp_ps[pi][:, 0:384]),
                    waits=[("t_pt", uu + 1)], inc="e_pv")
            for i, (j, kt, hg) in enumerate(TILES):
                first_of_j = (kt == 0 and hg == 0)
                if first_of_j:
                    iw = [("v_fi", 16 * j)] if j >= 1 else []
                    p.add("gpsimd", lambda e: e.memset(m_bufs[:], MINF), waits=iw)
                    p.add("gpsimd", lambda e: e.memset(l_buf[:], 0.0))
                    p.add("gpsimd", lambda e: e.memset(
                        O_sb[:].rearrange("p h d -> p (h d)"), 0.0), inc="a_in")
                if i + 2 < NTILE:
                    emit_kv(i + 2)
                masked = (kt == j)
                last_tile_of_j = (kt == j and hg == 1)
                for h8 in range(8):
                    h = hg * 8 + h8
                    pi = u % 2
                    ko = kt % 2
                    kn = (kt + 1) % 2

                    if u + 1 < NU:
                        emit_A(u + 1)

                    vw = [("t_s", u + 1)]
                    if first_of_j and h8 == 0:
                        vw.append(("a_in", j + 1))
                    if masked:
                        vw.append(("d_m%d" % (j % 4), 16 * (j // 4 + 1)))
                        p.add("vector", lambda e, pi=pi, j=j: e.scalar_tensor_tensor(
                            s_ps[pi][:], s_ps[pi][:], 1.0, mask_sb[j % 4][:],
                            ALU.mult, ALU.add), waits=vw)
                        vw = []
                    p.add("vector", lambda e, pi=pi: e.reduce_max(
                        c3(pi), s_ps[pi][:], axis=AX.X), waits=vw)
                    bubble_v(p)
                    p.add("vector", lambda e, h=h, ko=ko, kn=kn, pi=pi: e.tensor_tensor(
                        m_bufs[:, h, kn:kn + 1], m_bufs[:, h, ko:ko + 1], c3(pi), ALU.max))
                    bubble_v(p)
                    p.add("vector", lambda e, h=h, ko=ko, kn=kn, pi=pi: e.tensor_tensor(
                        c3(2 + pi), m_bufs[:, h, ko:ko + 1], m_bufs[:, h, kn:kn + 1],
                        ALU.subtract))
                    p.add("vector", lambda e, h=h, kn=kn: e.tensor_scalar(
                        nm_buf[:, h:h + 1], m_bufs[:, h, kn:kn + 1], -SCALE, None, ALU.mult))
                    bubble_v(p, inc="v_mx")
                    if u >= 3 and g_done < u - 3:
                        emit_G1(u - 3)
                        emit_G2(u - 3)
                        g_done = u - 3

                    if u >= 2 and ef_done < u - 2:
                        emit_EF(u - 2)
                        ef_done = u - 2

                    p.add("scalar", lambda e, pi=pi, h=h: e.activation(
                        out=c_buf[:, h:h + 1], in_=c3(2 + pi), func=AF.Exp, scale=SCALE),
                        waits=[("v_mx", u + 1)])
                    p.add("scalar", lambda e, pi=pi, h=h: e.activation(
                        out=p_sb[pi][:], in_=s_ps[pi][:], func=AF.Exp, scale=SCALE,
                        bias=nm_buf[:, h:h + 1], accum_out=rs_buf[:, h:h + 1]),
                        inc="s_ex")

                    if u >= 1 and d_done < u - 1:
                        emit_D(u - 1)
                        d_done = u - 1
                    u += 1

                if last_tile_of_j:
                    if d_done < u - 1:
                        emit_D(u - 1)
                        d_done = u - 1
                    while ef_done < u - 1:
                        emit_EF(ef_done + 1)
                        ef_done += 1
                    while g_done < u - 1:
                        emit_G1(g_done + 1)
                        emit_G2(g_done + 1)
                        g_done += 1
                    if j + 4 < JT:
                        emit_mask(j + 4)
                    for h in range(H + 1):
                        if h < H:
                            hb = h % 2
                            fw = [("v_ou", u)] if h == 0 else []
                            p.add("vector", lambda e, h=h, hb=hb: e.reciprocal(
                                c3(4 + hb), l_buf[:, h:h + 1]), waits=fw)
                        if h >= 1:
                            hp = h - 1
                            hb = hp % 2
                            p.add("vector", lambda e, hp=hp, hb=hb: e.tensor_scalar(
                                ofin[hb][:], O_sb[:, hp, :], c3(4 + hb), None, ALU.mult),
                                inc="v_fi")
                            tw = [("v_fi", 16 * j + hp + 1)]
                            if hp == 0:
                                tw.append(("e_pt", p.val("e_pt")))
                                if P3_EV_SPLIT:
                                    tw.append(("e_pv", p.val("e_pv")))
                            gfi = 16 * j + hp
                            if gfi >= 2:
                                tw.append(("v_f2", gfi - 1))
                            p.add("tensor", lambda e, hb=hb: e.transpose(
                                ptp_ps[hb][:, 0:128], ofin[hb][:], ident_bf[:]),
                                waits=tw, inc="t_fi")
                            p.add("vector", lambda e, hp=hp, j=j, hb=hb: e.tensor_copy(
                                o_fm[:, hp, j, :], ptp_ps[hb][:, 0:128]),
                                waits=[("t_fi", 16 * j + hp + 1)], inc="v_f2")

            p.add("sync", waits=[("d_k", p.val("d_k")), ("d_kb", p.val("d_kb"))])
            p.add("gpsimd", waits=[("d_v", p.val("d_v")), ("d_vb", p.val("d_vb")),
                                   ("d_m0", p.val("d_m0")), ("d_m1", p.val("d_m1")),
                                   ("d_m2", p.val("d_m2")), ("d_m3", p.val("d_m3"))])
            if "ofm" in debug:
                p.add("sync", lambda e: e.dma_start(out=dbg["dbg_ofm"], in_=o_fm[:]),
                      waits=[("v_f2", p.val("v_f2"))], inc="d_misc", inc_by=16)
                p.add("sync", waits=[("d_misc", p.val("d_misc"))])
            _run_block(nc, "p3", p, sems)
        if STOP_AFTER == "p3":
            qfm_ctx.close()
            ofm_ctx.close()
            return nc

        qfm_ctx.close()

        with ExitStack() as P4s:
            y_sb = P4s.enter_context(nc.sbuf_tensor("y_sb", [128, JT, D], F32))
            wbigQ = [P4s.enter_context(nc.sbuf_tensor(f"wbq{i}", [128, 16, 512], BF16))
                     for i in range(2)]
            yln_fm = P4s.enter_context(nc.sbuf_tensor("yln_fm", [128, 16, JT * 128], BF16))
            xt_sb = [P4s.enter_context(nc.sbuf_tensor(f"xt{i}", [128, 1024], F32)) for i in range(2)]
            ybf0 = P4s.enter_context(nc.sbuf_tensor("ybf", [128, D], BF16))
            ybf = [ybf0, ybf0]
            fs_sb = P4s.enter_context(nc.sbuf_tensor("fs_sb", [128, D], F32))
            fb_sb = P4s.enter_context(nc.sbuf_tensor("fb_sb", [128, D], F32))
            tmp0 = P4s.enter_context(nc.sbuf_tensor("tm0", [128, 512], F32))
            tmp_sb = [tmp0, tmp0]
            dump4 = ybf0
            sc4 = P4s.enter_context(nc.sbuf_tensor("sc4", [128, 16], F32))

            mm_ps = [P4s.enter_context(nc.psum_tensor(f"mm{i}", [128, 512], F32)) for i in range(2)]
            tp4_ps = [P4s.enter_context(nc.psum_tensor(f"tp4{i}", [128, 1024], BF16)) for i in range(2)]

            def c4(i):
                return sc4[:, i:i + 1]

            p = Prog(cnt)
            for half in range(2):
                for nq in range(2):
                    qcol = half * 1024 + nq * 512
                    lw = [] if half == 0 else [("w_mm", 15 + nq)]
                    p.add("gpsimd", lambda e, qcol=qcol, nq=nq: e.dma_start(
                        out=wbigQ[nq][:],
                        in_=woT_d.rearrange("(c p) n -> p c n", p=128)[
                            :, :, qcol:qcol + 512]),
                        waits=lw, inc="d_w" if nq == 0 else "d_wb", inc_by=16)
                wrdy0 = p.val("d_w")
                wrdy1 = p.val("d_wb")
                for j in range(JT):
                    idx = 8 * half + j
                    p.add("sync", lambda e, half=half, j=j, idx=idx: e.dma_start(
                        out=xt_sb[idx % 2][:],
                        in_=xtok_d[j, :, half * 1024:(half + 1) * 1024]),
                        waits=[("w_ev", 2 * (idx - 1))] if idx >= 2 else [],
                        inc="d_xs" if idx % 2 == 0 else "d_xsb", inc_by=16)
                    for n in range(2):
                        gi = 2 * idx + n
                        nb = gi % 2

                        def womm(e, j=j, n=n, nb=nb):
                            last = None
                            for ch in range(16):
                                last = e.matmul(mm_ps[nb][:],
                                                o_fm[:, ch, j, :],
                                                wbigQ[n][:, ch, :],
                                                start=(ch == 0), stop=(ch == 15))
                            return last
                        mw = [("d_w", wrdy0) if n == 0 else ("d_wb", wrdy1)]
                        if gi >= 2:
                            mw.append(("w_ev", gi - 1))
                        p.add("tensor", womm, waits=mw, inc="w_mm")
                        p.add("vector", lambda e, j=j, half=half, n=n, nb=nb, idx=idx:
                              e.scalar_tensor_tensor(
                                  y_sb[:, j, half * 1024 + n * 512:half * 1024 + (n + 1) * 512],
                                  mm_ps[nb][:], 1.0,
                                  xt_sb[idx % 2][:, n * 512:(n + 1) * 512], ALU.mult, ALU.add),
                              waits=[("w_mm", gi + 1),
                                     ("d_xs" if idx % 2 == 0 else "d_xsb",
                                      16 * (idx // 2 + 1))],
                              inc="w_ev")

            def emit_ln(j, phase, ev_sem, ev_need, out_dma):
                pi = j % 2
                lw = [(ev_sem, ev_need)]
                sqw = list(lw)
                if phase == 0 and j >= 1:
                    sqw.append(("y_tp", 4 * j))
                p.add("vector", lambda e, j=j, pi=pi: e.reduce_sum(
                    c4(0 + pi), y_sb[:, j, :], axis=AX.X), waits=lw)
                p.add("scalar", lambda e, j=j, pi=pi: e.activation(
                    out=dump4[:], in_=y_sb[:, j, :], func=AF.Square, accum_out=c4(2 + pi)),
                    waits=sqw, inc="l_sm")
                sm_need = p.val("l_sm")
                bubble_v(p)
                p.add("vector", lambda e, pi=pi: e.tensor_scalar(
                    c4(4 + pi), c4(0 + pi), -1.0 / D, None, ALU.mult))
                p.add("vector", lambda e, pi=pi: e.tensor_scalar(
                    c4(6 + pi), c4(2 + pi), 1.0 / D, EPS_LN, ALU.mult, ALU.add),
                    waits=[("l_sm", sm_need)])
                p.add("vector", lambda e, pi=pi: e.tensor_tensor(
                    c4(8 + pi), c4(4 + pi), c4(4 + pi), ALU.mult))
                bubble_v(p)
                p.add("vector", lambda e, pi=pi: e.scalar_tensor_tensor(
                    c4(10 + pi), c4(8 + pi), -1.0, c4(6 + pi), ALU.mult, ALU.add))
                bubble_v(p)
                p.add("vector", lambda e, pi=pi: e.reciprocal(c4(12 + pi), c4(10 + pi)))
                bubble_v(p, inc="y_ln")
                rc_need = p.val("y_ln")
                p.add("scalar", lambda e, pi=pi: e.activation(
                    out=c4(14 + pi), in_=c4(12 + pi), func=AF.Sqrt),
                    waits=[("y_ln", rc_need)])
                bubble_s(p, AF.Sqrt, inc="l_sq")
                sq_need = p.val("l_sq")
                p.add("vector", lambda e, j=j, pi=pi: e.tensor_scalar(
                    y_sb[:, j, :], y_sb[:, j, :], c4(4 + pi), c4(14 + pi), ALU.add, ALU.mult),
                    waits=[("l_sq", sq_need)],
                    inc="z_ln" if out_dma else None)
                if out_dma:
                    p.add("sync", lambda e, j=j: e.dma_start(
                        out=out_d[j, :, :], in_=y_sb[:, j, :]),
                        waits=[("z_ln", p.val("z_ln"))], inc="d_out", inc_by=16)

            for j in range(JT):
                pi = j % 2
                emit_ln(j, 0, "w_ev", 16 + 2 * (j + 1), False)
                cw = [("y_ln", p.val("y_ln"))]
                if j >= 1:
                    cw.append(("y_tp", 4 * j))
                p.add("vector", lambda e, j=j, pi=pi: e.tensor_copy(ybf[pi][:], y_sb[:, j, :]),
                      waits=cw, inc="y_ln")
                ybf_need = p.val("y_ln")
                for r in range(4):
                    ridx = 4 * j + r
                    rb = ridx % 2

                    def ytp(e, pi=pi, r=r, rb=rb):
                        last = None
                        for q in range(4):
                            last = e.transpose(tp4_ps[rb][:, q * 128:(q + 1) * 128],
                                               ybf[pi][:, (4 * r + q) * 128:(4 * r + q + 1) * 128],
                                               ident_bf[:])
                        return last
                    tw = [("y_ln", ybf_need)]
                    if ridx >= 2:
                        tw.append(("y_te", ridx - 1))
                    p.add("tensor", ytp, waits=tw, inc="y_tp")
                    p.add("vector", lambda e, j=j, r=r, rb=rb: e.tensor_copy(
                        yln_fm[:, 4 * r:4 * r + 4, j * 128:(j + 1) * 128],
                        tp4_ps[rb][:, 0:512].rearrange("p (c x) -> p c x", x=128)),
                        waits=[("y_tp", ridx + 1)], inc="y_te")

            p.add("sync", lambda e: e.dma_start(out=fs_sb[:], in_=fs_d[:, :]),
                  inc="d_misc", inc_by=16)
            p.add("sync", lambda e: e.dma_start(out=fb_sb[:], in_=fb_d[:, :]),
                  inc="d_misc", inc_by=16)
            fsrdy = p.val("d_misc")
            for j in range(JT):
                p.add("vector", lambda e, j=j: e.tensor_tensor(
                    y_sb[:, j, :], y_sb[:, j, :], fb_sb[:], ALU.add),
                    waits=[("y_te", 4 * JT), ("d_misc", fsrdy)] if j == 0 else ())
            for half in range(2):
                for nq in range(2):
                    qcol = half * 1024 + nq * 512
                    lw = [("w_mm", 31 + nq)] if half == 0 else [("f_mm", 15 + nq)]
                    p.add("gpsimd", lambda e, qcol=qcol, nq=nq: e.dma_start(
                        out=wbigQ[nq][:],
                        in_=wfT_d.rearrange("(c p) n -> p c n", p=128)[
                            :, :, qcol:qcol + 512]),
                        waits=lw, inc="d_w" if nq == 0 else "d_wb", inc_by=16)
                wrdy0 = p.val("d_w")
                wrdy1 = p.val("d_wb")
                for j in range(JT):
                    idx = 8 * half + j
                    for n in range(2):
                        gi = 2 * idx + n
                        nb = gi % 2
                        off = half * 1024 + n * 512

                        def ffmm(e, j=j, n=n, nb=nb):
                            last = None
                            for ch in range(16):
                                last = e.matmul(mm_ps[nb][:],
                                                yln_fm[:, ch, j * 128:(j + 1) * 128],
                                                wbigQ[n][:, ch, :],
                                                start=(ch == 0), stop=(ch == 15))
                            return last
                        mw = [("d_w", wrdy0) if n == 0 else ("d_wb", wrdy1)]
                        mw.append(("y_te", 4 * (j + 1)))
                        if gi >= 2:
                            mw.append(("f_ev", gi - 1))
                        else:
                            mw.append(("w_ev", 32))
                        p.add("tensor", ffmm, waits=mw, inc="f_mm")
                        p.add("vector", lambda e, nb=nb, off=off: e.tensor_tensor(
                            tmp_sb[nb][:], mm_ps[nb][:], fs_sb[:, off:off + 512], ALU.mult),
                            waits=[("f_mm", gi + 1)])
                        p.add("vector", lambda e, j=j, nb=nb, off=off: e.tensor_tensor(
                            y_sb[:, j, off:off + 512], y_sb[:, j, off:off + 512],
                            tmp_sb[nb][:], ALU.add), inc="f_ev")
            for j in range(JT):
                emit_ln(j, 1, "f_ev", 16 + 2 * (j + 1), True)

            if "yln" in debug:
                p.add("sync", lambda e: e.dma_start(out=dbg["dbg_yln"], in_=y_sb[:]),
                      waits=[("y_te", 32)], inc="d_misc", inc_by=16)
            p.add("gpsimd", waits=[("d_w", p.val("d_w"))])
            p.add("sync", waits=[("d_out", p.val("d_out")), ("d_misc", p.val("d_misc")),
                                 ("d_xs", p.val("d_xs")), ("d_xsb", p.val("d_xsb"))])
            _run_block(nc, "p45", p, sems)

        ofm_ctx.close()
    return nc



def _host_prepare(inputs):
    x = np.ascontiguousarray(np.asarray(inputs["L0_attn_hidden_states"], dtype=np.float32)[0])
    w_qa = np.asarray(inputs["L0_attn_w_qa"], dtype=np.float32)
    q_a_ln_w = np.asarray(inputs["L0_attn_q_a_ln_w"], dtype=np.float32)
    w_qb = np.asarray(inputs["L0_attn_w_qb"], dtype=np.float32)
    w_kva = np.asarray(inputs["L0_attn_w_kva"], dtype=np.float32)
    kv_a_ln_w = np.asarray(inputs["L0_attn_kv_a_ln_w"], dtype=np.float32)
    w_kvb = np.asarray(inputs["L0_attn_w_kvb"], dtype=np.float32)
    w_o = np.asarray(inputs["L0_attn_w_o"], dtype=np.float32)
    f_w = np.asarray(inputs["L0_ffn_weight"], dtype=np.float32)
    f_s = np.asarray(inputs["L0_ffn_scale_weight"], dtype=np.float32)
    f_b = np.asarray(inputs["L0_ffn_bias"], dtype=np.float32)

    C = np.ascontiguousarray
    shared = {
        "wqaT": C(w_qa.T),
        "wqbT": C(w_qb.T),
        "wkvaT": C(w_kva.T),
        "wkbkT": C(w_kvb.reshape(H, DN + DV, KVLR)[:, :DN].reshape(H * DN, KVLR).T),
        "wkbvT": C(w_kvb.reshape(H, DN + DV, KVLR)[:, DN:].reshape(H * DV, KVLR).T),
        "woT": C(w_o.T),
        "wfT": C(f_w.T),
        "gq": C(np.broadcast_to(q_a_ln_w, (128, QLR))),
        "gkv": C(np.broadcast_to(kv_a_ln_w, (128, KVLR))),
        "fs": C(np.broadcast_to(f_s, (128, D))),
        "fb": C(np.broadcast_to(f_b, (128, D))),
        "ident": np.eye(128, dtype=np.float32),
    }
    pos = np.arange(S, dtype=np.float32)
    inv_freq = 1.0 / (10000.0 ** (np.arange(0, DR, 2, dtype=np.float32) / DR))
    freqs = np.outer(pos, inv_freq)
    emb = np.concatenate([freqs, freqs], axis=1)
    cos = np.cos(emb).astype(np.float32)
    sin = np.sin(emb).astype(np.float32)
    sinS = np.concatenate([-sin[:, :DR // 2], sin[:, DR // 2:]], axis=1)
    cosk_t = cos.reshape(NT, 128, DR)
    sink_t = sinS.reshape(NT, 128, DR)

    xt = x.reshape(NT, 128, D)
    in_maps = []
    for c in range(NCORES):
        tiles = [8 * j + c for j in range(JT)]
        xc = xt[tiles]
        cosq = np.tile(cos.reshape(NT, 128, 1, DR)[tiles], (1, 1, H, 1))
        sinq = np.tile(sinS.reshape(NT, 128, 1, DR)[tiles], (1, 1, H, 1))
        mask = np.zeros((JT, 128, 1024), dtype=np.float32)
        for j in range(JT):
            t = 8 * j + c
            qpos = t * 128 + np.arange(128)[:, None]
            kpos = j * 1024 + np.arange(1024)[None, :]
            mask[j] = np.where(kpos <= qpos, 0.0, NEGBIG)
        m = dict(shared)
        m["xtok"] = C(xc)
        m["xT_own"] = C(xc.reshape(JT * 128, D).T)
        m["xT_kv"] = C(x[1024 * c:1024 * (c + 1)].T)
        m["cosk"] = C(cosk_t[8 * c:8 * c + 8])
        m["sink"] = C(sink_t[8 * c:8 * c + 8])
        m["cosq"] = C(cosq.reshape(JT, 128, H * DR))
        m["sinq"] = C(sinq.reshape(JT, 128, H * DR))
        m["mask"] = mask
        in_maps.append(m)
    return in_maps


def kernel(**inputs):
    global LAST_RESULTS
    in_maps = _host_prepare(inputs)
    nc = build_program(debug=DEBUG)
    res = run_bass_kernel_spmd(nc, in_maps, core_ids=list(range(NCORES)))
    if DEBUG:
        LAST_RESULTS = res
    y = np.empty((NT, 128, D), dtype=np.float32)
    for c in range(NCORES):
        o = np.asarray(res.results[c]["out"])
        for j in range(JT):
            y[8 * j + c] = o[j]
    return y.reshape(1, S, D)


# version 24
# speedup vs baseline: 1.0280x; 1.0280x over previous
import numpy as np
from contextlib import ExitStack

import concourse.bass as bass
import concourse.mybir as mybir
from concourse.bass_utils import run_bass_kernel_spmd

F32 = mybir.dt.float32
BF16 = mybir.dt.bfloat16
AF = mybir.ActivationFunctionType
ALU = mybir.AluOpType
AX = mybir.AxisListType

NCORES = 8
S = 8192
D = 2048
QLR = 1536
KVLR = 512
H = 16
DN = 128
DR = 64
DV = 128
DQ = DN + DR
SCALE = float(DQ) ** -0.5
EPS_RMS = 1e-6
EPS_LN = 1e-5
NEGBIG = -1.0e30
MINF = -3.0e38

JT = 8
NT = 64
NG = 16
NGB = 2

ENGS = ("tensor", "vector", "scalar", "sync", "gpsimd")

DEBUG = set()
STOP_AFTER = None
P1_EVAC_V_ONLY = True
P1_TPS_LAG = True
P3_EV_SPLIT = True


class Prog:
    def __init__(self, cnt):
        self.ops = {e: [] for e in ENGS}
        self.cnt = cnt

    def add(self, eng, fn=None, waits=(), inc=None, inc_by=1):
        if inc is not None:
            self.cnt[inc] = self.cnt.get(inc, 0) + inc_by
        self.ops[eng].append((fn, tuple(waits), inc, inc_by))

    def val(self, sem):
        return self.cnt.get(sem, 0)


def _replay(prog, eng_obj, eng_name, sems):
    for fn, waits, inc, inc_by in prog.ops[eng_name]:
        for s, v in waits:
            if v > 0:
                eng_obj.wait_ge(sems[s], v)
        instr = fn(eng_obj) if fn is not None else None
        if inc is not None:
            assert instr is not None
            instr.then_inc(sems[inc], inc_by)


def _run_block(nc, name, prog, sems):
    with nc.Block(name, no_gpsimd_drain=True) as blk:
        for en in ENGS:
            getattr(blk, en)(lambda e, en=en: _replay(prog, e, en, sems))


SEM_NAMES = [
    "d_w", "d_x", "d_kout", "d_vout", "d_k", "d_v", "d_m", "d_out", "d_misc",
    "p1_mm", "p1_ev", "p1_sq", "p1_rc", "p1_rt", "p1_ap", "p1_tp", "p1_te",
    "p1_km", "p1_ke", "p1_vm", "p1_ve",
    "p2_mm", "p2_ev", "p2_sq", "p2_rc", "p2_rt", "p2_ap", "p2_tp", "p2_te",
    "q_mm", "q_ev", "q_ro", "q_tp", "q_te",
    "a_in", "t_s", "v_mx", "s_ex", "t_pt", "e_pt", "t_pv", "v_ou",
    "v_fi", "t_fi", "v_f2",
    "w_mm", "w_ev", "y_ln", "l_sm", "l_sq", "y_tp", "y_te",
    "f_mm", "f_ev", "z_ln", "cc", "e_pv", "d_xs",
    "d_xb", "d_kb", "d_vb", "d_m0", "d_m1", "d_m2", "d_m3", "d_voutb", "d_xsb",
    "ccd", "d_wb", "d_wc",
]


def build_program(debug=()):
    nc = bass.Bass("TRN2", target_bir_lowering=False)
    debug = set(debug)

    xtok_d = nc.dram_tensor("xtok", [JT, 128, D], F32, kind="ExternalInput").ap()
    xT_own_d = nc.dram_tensor("xT_own", [D, JT * 128], F32, kind="ExternalInput").ap()
    xT_kv_d = nc.dram_tensor("xT_kv", [D, JT * 128], F32, kind="ExternalInput").ap()
    wqaT_d = nc.dram_tensor("wqaT", [D, QLR], F32, kind="ExternalInput").ap()
    wqbT_d = nc.dram_tensor("wqbT", [QLR, H * DQ], F32, kind="ExternalInput").ap()
    wkvaT_d = nc.dram_tensor("wkvaT", [D, KVLR + DR], F32, kind="ExternalInput").ap()
    wkbkT_d = nc.dram_tensor("wkbkT", [KVLR, H * DN], F32, kind="ExternalInput").ap()
    wkbvT_d = nc.dram_tensor("wkbvT", [KVLR, H * DV], F32, kind="ExternalInput").ap()
    woT_d = nc.dram_tensor("woT", [H * DV, D], F32, kind="ExternalInput").ap()
    wfT_d = nc.dram_tensor("wfT", [D, D], F32, kind="ExternalInput").ap()
    gq_d = nc.dram_tensor("gq", [128, QLR], F32, kind="ExternalInput").ap()
    gkv_d = nc.dram_tensor("gkv", [128, KVLR], F32, kind="ExternalInput").ap()
    fs_d = nc.dram_tensor("fs", [128, D], F32, kind="ExternalInput").ap()
    fb_d = nc.dram_tensor("fb", [128, D], F32, kind="ExternalInput").ap()
    cosq_d = nc.dram_tensor("cosq", [JT, 128, H * DR], F32, kind="ExternalInput").ap()
    sinq_d = nc.dram_tensor("sinq", [JT, 128, H * DR], F32, kind="ExternalInput").ap()
    cosk_d = nc.dram_tensor("cosk", [JT, 128, DR], F32, kind="ExternalInput").ap()
    sink_d = nc.dram_tensor("sink", [JT, 128, DR], F32, kind="ExternalInput").ap()
    mask_d = nc.dram_tensor("mask", [JT, 128, 1024], F32, kind="ExternalInput").ap()
    ident_d = nc.dram_tensor("ident", [128, 128], F32, kind="ExternalInput").ap()
    out_d = nc.dram_tensor("out", [JT, 128, D], F32, kind="ExternalOutput").ap()

    kv_kind = "ExternalOutput" if "kv" in debug else "Internal"
    kcon = nc.dram_tensor("kcon", [H, DN, JT * 128], BF16, kind="Internal")
    vcon = nc.dram_tensor("vcon", [H, 128, JT * DV], BF16, kind="Internal")
    kpecon = nc.dram_tensor("kpecon", [DR, JT * 128], BF16, kind="Internal")
    kg = nc.dram_tensor("kg", [NCORES * H, DN, JT * 128], BF16, kind=kv_kind,
                        addr_space="Shared")
    vg = nc.dram_tensor("vg", [NCORES * H, 128, JT * DV], BF16, kind=kv_kind,
                        addr_space="Shared")
    kpeg = nc.dram_tensor("kpeg", [NCORES * DR, JT * 128], BF16, kind="Internal",
                          addr_space="Shared")
    kg_v = kg.ap().rearrange("(c h) p k -> c h p k", c=NCORES)
    vg_v = vg.ap().rearrange("(c h) p n -> c h p n", c=NCORES)
    kpeg_v = kpeg.ap().rearrange("(c d) k -> c d k", c=NCORES)
    dbg = {}
    if "qfm" in debug:
        dbg["dbg_q0"] = nc.dram_tensor("dbg_q0", [128, H, JT, 128], BF16, kind="ExternalOutput").ap()
        dbg["dbg_q1"] = nc.dram_tensor("dbg_q1", [DR, H, JT, 128], BF16, kind="ExternalOutput").ap()
    if "ofm" in debug:
        dbg["dbg_ofm"] = nc.dram_tensor("dbg_ofm", [128, H, JT, 128], BF16, kind="ExternalOutput").ap()
    if "yln" in debug:
        dbg["dbg_yln"] = nc.dram_tensor("dbg_yln", [128, JT, D], F32, kind="ExternalOutput").ap()
    if "kpe" in debug:
        dbg["dbg_kpe"] = nc.dram_tensor("dbg_kpe", [DR, S], BF16, kind="ExternalOutput").ap()

    cnt = {}
    with ExitStack() as top:
        sems = {n: top.enter_context(nc.semaphore(n)) for n in SEM_NAMES}
        ident_bf = top.enter_context(nc.sbuf_tensor("ident_bf", [128, 128], BF16))
        sc = top.enter_context(nc.sbuf_tensor("sc", [128, 8], F32))
        jk_v = top.enter_context(nc.sbuf_tensor("jk_v", [128, 2], F32))
        jk_s = top.enter_context(nc.sbuf_tensor("jk_s", [128, 2], F32))
        ofm_ctx = ExitStack()
        o_fm = ofm_ctx.enter_context(nc.sbuf_tensor("o_fm", [128, H, JT, 128], BF16))
        kpe_ctx = ExitStack()
        kpe_fm = kpe_ctx.enter_context(nc.sbuf_tensor("kpe_fm", [DR, S], BF16))

        def bubble_v(p, inc=None):
            p.add("vector", lambda e: e.tensor_copy(jk_v[:, 0:1], jk_v[:, 1:2]), inc=inc)

        def bubble_s(p, func, inc=None):
            p.add("scalar", lambda e, func=func: e.activation(
                out=jk_s[:, 0:1], in_=jk_s[:, 1:2], func=func), inc=inc)

        with ExitStack() as P1s:
            gkv_sb = P1s.enter_context(nc.sbuf_tensor("gkv_sb", [128, KVLR], F32))
            wkva_sb = P1s.enter_context(nc.sbuf_tensor("wkva_sb", [128, 16, KVLR + DR], BF16))
            wkbk_sb = P1s.enter_context(nc.sbuf_tensor("wkbk_sb", [128, 4, H * DN], BF16))
            wkbv_sb = P1s.enter_context(nc.sbuf_tensor("wkbv_sb", [128, 4, H * DV], BF16))
            cosk_sb = P1s.enter_context(nc.sbuf_tensor("cosk_sb", [128, JT, DR], F32))
            sink_sb = P1s.enter_context(nc.sbuf_tensor("sink_sb", [128, JT, DR], F32))
            kpe_own = P1s.enter_context(nc.sbuf_tensor("kpe_own", [DR, JT * 128], BF16))
            xg_sb = [P1s.enter_context(nc.sbuf_tensor(f"xg{i}", [128, 16, 512], BF16)) for i in range(2)]
            ckv_sb = [P1s.enter_context(nc.sbuf_tensor(f"ckv{i}", [128, KVLR], F32)) for i in range(2)]
            kperaw = [P1s.enter_context(nc.sbuf_tensor(f"kpr{i}", [128, DR], F32)) for i in range(2)]
            t1k = P1s.enter_context(nc.sbuf_tensor("t1k", [128, DR], F32))
            avk = P1s.enter_context(nc.sbuf_tensor("avk", [128, DR], F32))
            kpe_bf = [P1s.enter_context(nc.sbuf_tensor(f"kpb{i}", [128, DR], BF16)) for i in range(2)]
            ckvn_tm = [P1s.enter_context(nc.sbuf_tensor(f"cvn{i}", [128, KVLR], BF16)) for i in range(2)]
            ckvn_g = [P1s.enter_context(nc.sbuf_tensor(f"cvg{i}", [128, 4, 512], BF16)) for i in range(2)]
            kOut = [P1s.enter_context(nc.sbuf_tensor(f"kO{i}", [128, H, 512], BF16)) for i in range(2)]
            vOut = [P1s.enter_context(nc.sbuf_tensor(f"vO{i}", [128, H * DV], BF16)) for i in range(2)]
            dump1 = P1s.enter_context(nc.sbuf_tensor("dump1", [128, 512], BF16))

            ckv_ps = P1s.enter_context(nc.psum_tensor("ckv_ps", [128, KVLR + DR], F32))
            tp_ps = [P1s.enter_context(nc.psum_tensor(f"tp{i}", [128, 1024], BF16)) for i in range(2)]
            kx_ps = [P1s.enter_context(nc.psum_tensor(f"kx{i}", [128, 512], F32)) for i in range(2)]
            vx_ps = [P1s.enter_context(nc.psum_tensor(f"vx{i}", [128, 512], F32)) for i in range(2)]

            def c1(i):
                return sc[:, i:i + 1]

            p = Prog(cnt)
            p.add("gpsimd", lambda e: e.memset(jk_v[:], 0.0))
            p.add("gpsimd", lambda e: e.memset(jk_s[:], 0.0))
            p.add("gpsimd", lambda e: e.dma_start(out=ident_bf[:], in_=ident_d[:, :]),
                  inc="d_w", inc_by=16)
            p.add("gpsimd", lambda e: e.dma_start(
                out=wkva_sb[:], in_=wkvaT_d.rearrange("(c p) n -> p c n", p=128)),
                inc="d_w", inc_by=16)
            p.add("gpsimd", lambda e: e.dma_start(
                out=wkbk_sb[:], in_=wkbkT_d.rearrange("(c p) n -> p c n", p=128)),
                inc="d_w", inc_by=16)
            p.add("gpsimd", lambda e: e.dma_start(
                out=wkbv_sb[:], in_=wkbvT_d.rearrange("(c p) n -> p c n", p=128)),
                inc="d_w", inc_by=16)
            p.add("sync", lambda e: e.dma_start(
                out=cosk_sb[:], in_=cosk_d.rearrange("t p d -> p t d")), inc="d_misc", inc_by=16)
            p.add("sync", lambda e: e.dma_start(
                out=sink_sb[:], in_=sink_d.rearrange("t p d -> p t d")), inc="d_misc", inc_by=16)
            p.add("sync", lambda e: e.dma_start(out=gkv_sb[:], in_=gkv_d[:, :]),
                  inc="d_misc", inc_by=16)
            w_ready = p.val("d_w")
            gkv_ready = p.val("d_misc")

            def emit_xg(g):
                b = g % 2
                sem = "d_x" if b == 0 else "d_xb"
                waits = []
                if g >= 2:
                    waits.append(("p1_mm", 4 * (g - 1)))
                p.add("gpsimd", lambda e, g=g, b=b: e.dma_start(
                    out=xg_sb[b][:],
                    in_=xT_kv_d.rearrange("(c p) t -> p c t", p=128)[:, :, g * 512:(g + 1) * 512]),
                    waits=waits, inc=sem, inc_by=16)

            emit_xg(0)
            emit_xg(1)

            tps_done = -1

            def emit_tps_evac(t):
                pi = t % 2
                g2 = t // 4
                gb2 = g2 % 2
                tt2 = t % 4

                def tps(e, pi=pi):
                    last = None
                    for chn in range(4):
                        last = e.transpose(tp_ps[pi][:, chn * 128:(chn + 1) * 128],
                                           ckvn_tm[pi][:, chn * 128:(chn + 1) * 128],
                                           ident_bf[:])
                    return e.transpose(tp_ps[pi][0:DR, 512:640], kpe_bf[pi][:], ident_bf[:])
                tw = [("p1_ap", t + 1)]
                if t >= 2:
                    tw.append(("p1_te", t - 1))
                p.add("tensor", tps, waits=tw, inc="p1_tp")

                ew2 = [("p1_tp", t + 1)]
                if g2 >= 2 and tt2 == 0:
                    ew2 += [("p1_km", 16 * (g2 - 1)), ("p1_vm", 16 * (g2 - 1))]
                p.add("vector", lambda e, pi=pi, gb2=gb2, tt2=tt2: e.tensor_copy(
                    ckvn_g[gb2][:, :, tt2 * 128:(tt2 + 1) * 128],
                    tp_ps[pi][:, 0:512].rearrange("p (c x) -> p c x", x=128)),
                    waits=ew2)
                p.add("vector", lambda e, pi=pi, t=t: e.tensor_copy(
                    kpe_own[:, t * 128:(t + 1) * 128], tp_ps[pi][0:DR, 512:640]),
                    inc="p1_te")

            for g in range(NGB):
                if g + 2 < NGB:
                    emit_xg(g + 2)
                gb = g % 2
                for tt_ in range(4):
                    t = 4 * g + tt_
                    pi = t % 2

                    def ckv_mms(e, gb=gb, tt_=tt_):
                        last = None
                        for ch in range(16):
                            last = e.matmul(ckv_ps[:, 0:512],
                                            xg_sb[gb][:, ch, tt_ * 128:(tt_ + 1) * 128],
                                            wkva_sb[:, ch, 0:512],
                                            start=(ch == 0), stop=(ch == 15),
                                            skip_group_check=True)
                        for ch in range(16):
                            last = e.matmul(ckv_ps[:, 512:576],
                                            xg_sb[gb][:, ch, tt_ * 128:(tt_ + 1) * 128],
                                            wkva_sb[:, ch, 512:576],
                                            start=(ch == 0), stop=(ch == 15),
                                            skip_group_check=True)
                        return last
                    waits = [("d_x" if g % 2 == 0 else "d_xb", 16 * (g // 2 + 1)),
                             ("d_w", w_ready), ("d_misc", gkv_ready)]
                    if t >= 1:
                        waits.append(("p1_ev", t))
                    p.add("tensor", ckv_mms, waits=waits, inc="p1_mm")

                    ew = [("p1_mm", t + 1)]
                    if t >= 2:
                        ew.append(("p1_sq", t - 1))
                    p.add("vector", lambda e, pi=pi: e.tensor_copy(ckv_sb[pi][:], ckv_ps[:, 0:512]),
                          waits=ew)
                    p.add("vector", lambda e, pi=pi: e.tensor_copy(kperaw[pi][:], ckv_ps[:, 512:576]),
                          inc="p1_ev")
                    p.add("scalar", lambda e, pi=pi: e.activation(
                        out=dump1[:], in_=ckv_sb[pi][:], func=AF.Square,
                        accum_out=c1(0 + pi)),
                        waits=[("p1_ev", t + 1)], inc="p1_sq")
                    p.add("vector", lambda e, pi=pi: e.tensor_scalar(
                        c1(2 + pi), c1(0 + pi), 1.0 / KVLR, EPS_RMS, ALU.mult, ALU.add),
                        waits=[("p1_sq", t + 1)])
                    bubble_v(p)
                    p.add("vector", lambda e, pi=pi: e.reciprocal(c1(4 + pi), c1(2 + pi)))
                    bubble_v(p, inc="p1_rc")
                    p.add("scalar", lambda e, pi=pi: e.activation(
                        out=c1(6 + pi), in_=c1(4 + pi), func=AF.Sqrt),
                        waits=[("p1_rc", t + 1)])
                    bubble_s(p, AF.Sqrt, inc="p1_rt")
                    p.add("vector", lambda e, pi=pi: e.scalar_tensor_tensor(
                        ckvn_tm[pi][:], ckv_sb[pi][:], c1(6 + pi), gkv_sb[:],
                        ALU.mult, ALU.mult),
                        waits=[("p1_rt", t + 1), ("d_misc", gkv_ready)])
                    p.add("vector", lambda e, pi=pi: e.tensor_copy(t1k[:, 0:32], kperaw[pi][:, 32:64]))
                    p.add("vector", lambda e, pi=pi: e.tensor_copy(t1k[:, 32:64], kperaw[pi][:, 0:32]))
                    p.add("vector", lambda e, pi=pi, t=t: e.tensor_tensor(
                        avk[:], kperaw[pi][:], cosk_sb[:, t, :], ALU.mult))
                    p.add("vector", lambda e, t=t: e.tensor_tensor(
                        t1k[:], t1k[:], sink_sb[:, t, :], ALU.mult))
                    p.add("vector", lambda e, pi=pi: e.tensor_tensor(
                        kpe_bf[pi][:], avk[:], t1k[:], ALU.add), inc="p1_ap")

                    if P1_TPS_LAG:
                        if t >= 1 and tps_done < t - 1:
                            emit_tps_evac(t - 1)
                            tps_done = t - 1
                        if tt_ == 3:
                            emit_tps_evac(t)
                            tps_done = t
                    else:
                        emit_tps_evac(t)
                        tps_done = t

                for m in range(H):
                    mb = m % 2

                    def kmm(e, m=m, mb=mb, gb=gb):
                        last = None
                        for ch in range(4):
                            last = e.matmul(kx_ps[mb][:],
                                            wkbk_sb[:, ch, m * 128:(m + 1) * 128],
                                            ckvn_g[gb][:, ch, :],
                                            start=(ch == 0), stop=(ch == 3))
                        return last
                    kw = [("p1_te", 4 * (g + 1))]
                    if p.val("p1_ke") >= 2:
                        kw.append(("p1_ke", p.val("p1_ke") - 1))
                    p.add("tensor", kmm, waits=kw, inc="p1_km")
                    ew = [("p1_km", p.val("p1_km"))]
                    if g >= 2 and m == 0:
                        ew.append(("d_kout", 16 * (g - 1)))
                    if m % 2 == 0 or P1_EVAC_V_ONLY:
                        p.add("vector", lambda e, m=m, mb=mb, gb=gb: e.tensor_copy(
                            kOut[gb][:, m, :], kx_ps[mb][:]), waits=ew, inc="p1_ke")
                    else:
                        p.add("scalar", lambda e, m=m, mb=mb, gb=gb: e.copy(
                            out=kOut[gb][:, m, :], in_=kx_ps[mb][:]), waits=ew, inc="p1_ke")
                p.add("sync", lambda e, g=g, gb=gb: e.dma_start(
                    out=kcon.ap().rearrange("h p k -> p h k")[:, :, g * 512:(g + 1) * 512],
                    in_=kOut[gb][:]),
                    waits=[("p1_ke", p.val("p1_ke"))], inc="d_kout", inc_by=16)

                for tt_ in range(4):
                    t = 4 * g + tt_
                    vb = t % 2
                    for n in range(4):
                        nb = p.val("p1_vm") % 2

                        def vmm(e, tt_=tt_, n=n, nb=nb, gb=gb):
                            last = None
                            for ch in range(4):
                                last = e.matmul(vx_ps[nb][:],
                                                ckvn_g[gb][:, ch, tt_ * 128:(tt_ + 1) * 128],
                                                wkbv_sb[:, ch, n * 512:(n + 1) * 512],
                                                start=(ch == 0), stop=(ch == 3))
                            return last
                        vw = []
                        if p.val("p1_vm") >= 2:
                            vw.append(("p1_ve", p.val("p1_vm") - 1))
                        p.add("tensor", vmm, waits=vw, inc="p1_vm")
                        ew = [("p1_vm", p.val("p1_vm"))]
                        if t >= 2 and n == 0:
                            ew.append(("d_vout" if t % 2 == 0 else "d_voutb",
                                       p.val("d_vout" if t % 2 == 0 else "d_voutb")))
                        if n % 2 == 0 or P1_EVAC_V_ONLY:
                            p.add("vector", lambda e, vb=vb, n=n, nb=nb: e.tensor_copy(
                                vOut[vb][:, n * 512:(n + 1) * 512], vx_ps[nb][:]),
                                waits=ew, inc="p1_ve")
                        else:
                            p.add("scalar", lambda e, vb=vb, n=n, nb=nb: e.copy(
                                out=vOut[vb][:, n * 512:(n + 1) * 512], in_=vx_ps[nb][:]),
                                waits=ew, inc="p1_ve")
                    p.add("sync", lambda e, t=t, vb=vb: e.dma_start(
                        out=vcon.ap().rearrange("h p n -> p h n")[:, :,
                                                                  t * DV:(t + 1) * DV],
                        in_=vOut[vb][:].rearrange("p (h d) -> p h d", d=DV)),
                        waits=[("p1_ve", p.val("p1_ve"))],
                        inc="d_vout" if vb == 0 else "d_voutb", inc_by=16)

            p.add("sync", lambda e: e.dma_start(out=kpecon.ap(), in_=kpe_own[:]),
                  waits=[("p1_te", p.val("p1_te"))], inc="d_kout", inc_by=16)
            p.add("sync", waits=[("d_kout", p.val("d_kout")), ("d_vout", p.val("d_vout"))])
            p.add("gpsimd", waits=[("d_x", p.val("d_x")), ("d_xb", p.val("d_xb")),
                                   ("d_w", p.val("d_w")),
                                   ("d_kout", p.val("d_kout")),
                                   ("d_vout", p.val("d_vout")),
                                   ("d_voutb", p.val("d_voutb"))])
            if "kpe" in debug:
                p.add("sync", lambda e: e.dma_start(out=dbg["dbg_kpe"], in_=kpe_fm[:]),
                      waits=[("p1_te", p.val("p1_te"))], inc="d_misc", inc_by=16)
                p.add("sync", waits=[("d_misc", p.val("d_misc"))])
            _run_block(nc, "p1", p, sems)
            if STOP_AFTER == "p1":
                return nc

        qfm_ctx = ExitStack()
        q_fm0 = qfm_ctx.enter_context(nc.sbuf_tensor("q_fm0", [128, H, JT, 128], BF16))
        q_fm1 = qfm_ctx.enter_context(nc.sbuf_tensor("q_fm1", [DR, H, JT, 128], BF16))

        with ExitStack() as P2s:
            cqn_fm = P2s.enter_context(nc.sbuf_tensor("cqn_fm", [128, 12, JT * 128], BF16))
            cq_ps = [P2s.enter_context(nc.psum_tensor(f"cqp{i}", [128, 512], F32)) for i in range(3)]
            tp2_ps = [P2s.enter_context(nc.psum_tensor(f"tp2{i}", [128, 1024], BF16)) for i in range(2)]
            dump2 = P2s.enter_context(nc.sbuf_tensor("dump2", [128, 512], BF16))
            sc2 = P2s.enter_context(nc.sbuf_tensor("sc2", [128, 16], F32))

            def c2(i):
                return sc2[:, i:i + 1]

            p = Prog(cnt)

            with ExitStack() as P2a:
                gq_sb = P2a.enter_context(nc.sbuf_tensor("gq_sb", [128, QLR], F32))
                wqa_sb = P2a.enter_context(nc.sbuf_tensor("wqa_sb", [128, 16, QLR], BF16))
                xg2_0 = P2a.enter_context(nc.sbuf_tensor("xg2", [128, 16, 128], BF16))
                xg2 = [xg2_0, xg2_0]
                cq_sb = P2a.enter_context(nc.sbuf_tensor("cq_sb", [128, QLR], F32))
                cqn_tm0 = P2a.enter_context(nc.sbuf_tensor("cqn_tm", [128, QLR], BF16))
                cqn_tm = [cqn_tm0, cqn_tm0]

                p.add("gpsimd", lambda e: e.dma_start(
                    out=wqa_sb[:], in_=wqaT_d.rearrange("(c p) n -> p c n", p=128)),
                    inc="d_w", inc_by=16)
                p.add("sync", lambda e: e.dma_start(out=gq_sb[:], in_=gq_d[:, :]),
                      inc="d_misc", inc_by=16)
                wqa_ready = p.val("d_w")
                gq_ready = p.val("d_misc")
                x2base = p.val("d_x")

                def emit_x2(j):
                    waits = []
                    if j >= 1:
                        waits.append(("p2_mm", j))
                    p.add("gpsimd", lambda e, j=j: e.dma_start(
                        out=xg2[0][:],
                        in_=xT_own_d.rearrange("(c p) t -> p c t", p=128)[:, :, j * 128:(j + 1) * 128]),
                        waits=waits, inc="d_x", inc_by=16)

                emit_x2(0)
                for j in range(JT):
                    if j + 1 < JT:
                        emit_x2(j + 1)
                    pi = j % 2

                    def cq_mms(e, pi=pi):
                        last = None
                        for ch in range(16):
                            for n in range(3):
                                last = e.matmul(cq_ps[n][:],
                                                xg2[pi][:, ch, :],
                                                wqa_sb[:, ch, n * 512:(n + 1) * 512],
                                                start=(ch == 0), stop=(ch == 15),
                                                skip_group_check=True)
                        return last
                    waits = [("d_x", x2base + 16 * (j + 1)), ("d_w", wqa_ready)]
                    if j >= 1:
                        waits.append(("p2_ev", 3 * j))
                    p.add("tensor", cq_mms, waits=waits, inc="p2_mm")

                    for n in range(3):
                        ew = [("p2_mm", j + 1)]
                        if n == 0 and j >= 1:
                            ew.append(("p2_sq", j))
                        p.add("vector", lambda e, n=n: e.tensor_copy(
                            cq_sb[:, n * 512:(n + 1) * 512], cq_ps[n][:]),
                            waits=ew, inc="p2_ev")
                    sqw = [("p2_ev", 3 * (j + 1))]
                    if j >= 1:
                        sqw.append(("p2_tp", 3 * j))
                    p.add("scalar", lambda e, pi=pi: e.activation(
                        out=cqn_tm[pi][:], in_=cq_sb[:], func=AF.Square,
                        accum_out=c2(0 + pi)), waits=sqw, inc="p2_sq")
                    p.add("vector", lambda e, pi=pi: e.tensor_scalar(
                        c2(8 + pi), c2(0 + pi), 1.0 / QLR, EPS_RMS, ALU.mult, ALU.add),
                        waits=[("p2_sq", j + 1)])
                    bubble_v(p)
                    p.add("vector", lambda e, pi=pi: e.reciprocal(c2(10 + pi), c2(8 + pi)))
                    bubble_v(p, inc="p2_rc")
                    p.add("scalar", lambda e, pi=pi: e.activation(
                        out=c2(12 + pi), in_=c2(10 + pi), func=AF.Sqrt),
                        waits=[("p2_rc", j + 1)])
                    bubble_s(p, AF.Sqrt, inc="p2_rt")
                    aw = [("p2_rt", j + 1), ("d_misc", gq_ready)]
                    if j >= 1:
                        aw.append(("p2_tp", 3 * j))
                    for n in range(3):
                        p.add("vector", lambda e, pi=pi, n=n: e.scalar_tensor_tensor(
                            cqn_tm[pi][:, n * 512:(n + 1) * 512],
                            cq_sb[:, n * 512:(n + 1) * 512], c2(12 + pi),
                            gq_sb[:, n * 512:(n + 1) * 512], ALU.mult, ALU.mult),
                            waits=aw if n == 0 else (), inc="p2_ap" if n == 2 else None)
                    for r in range(3):
                        ridx = 3 * j + r
                        rb = ridx % 2

                        def tpr(e, pi=pi, r=r, rb=rb):
                            last = None
                            for q in range(4):
                                last = e.transpose(
                                    tp2_ps[rb][:, q * 128:(q + 1) * 128],
                                    cqn_tm[pi][:, (4 * r + q) * 128:(4 * r + q + 1) * 128],
                                    ident_bf[:])
                            return last
                        tw = [("p2_ap", j + 1)]
                        if ridx >= 2:
                            tw.append(("p2_te", ridx - 1))
                        p.add("tensor", tpr, waits=tw, inc="p2_tp")
                        p.add("vector", lambda e, r=r, rb=rb, j=j: e.tensor_copy(
                            cqn_fm[:, 4 * r:4 * r + 4, j * 128:(j + 1) * 128],
                            tp2_ps[rb][:, 0:512].rearrange("p (c x) -> p c x", x=128)),
                            waits=[("p2_tp", ridx + 1)], inc="p2_te")
                p.add("gpsimd", waits=[("d_w", p.val("d_w")), ("d_x", p.val("d_x"))])
                p.add("gpsimd", lambda e: e.collective_compute(
                    "AllGather", ALU.bypass, replica_groups=[list(range(NCORES))],
                    ins=[kcon.ap().opt()], outs=[kg.ap().opt()]), inc="cc")
                p.add("gpsimd", lambda e: e.collective_compute(
                    "AllGather", ALU.bypass, replica_groups=[list(range(NCORES))],
                    ins=[vcon.ap().opt()], outs=[vg.ap().opt()]), inc="cc")
                p.add("gpsimd", lambda e: e.collective_compute(
                    "AllGather", ALU.bypass, replica_groups=[list(range(NCORES))],
                    ins=[kpecon.ap().opt()], outs=[kpeg.ap().opt()]), inc="cc")
                p.add("gpsimd", lambda e: e.memset(jk_v[:], 0.0), inc="ccd")
                p.add("sync", waits=[("d_misc", p.val("d_misc"))])
                _run_block(nc, "p2a", p, sems)

            p = Prog(cnt)
            with ExitStack() as P2b:
                wqb_sb = P2b.enter_context(nc.sbuf_tensor("wqb_sb", [128, 12, 8 * DQ], BF16))
                q_tm = [P2b.enter_context(nc.sbuf_tensor(f"qtm{i}", [128, 8 * DQ], F32))
                        for i in range(2)]
                qbf = [P2b.enter_context(nc.sbuf_tensor(f"qbf{i}", [128, 8 * DQ], BF16))
                       for i in range(2)]
                t1q = P2b.enter_context(nc.sbuf_tensor("t1q", [128, 8 * DR], F32))
                avq = P2b.enter_context(nc.sbuf_tensor("avq", [128, 8 * DR], F32))
                cosq_sb = P2b.enter_context(nc.sbuf_tensor("cosq_sb", [128, 8 * DR * 2], F32))
                sinq_sb = P2b.enter_context(nc.sbuf_tensor("sinq_sb", [128, 8 * DR * 2], F32))

                for hh in range(2):
                    wqb_waits = [("q_mm", 8 * hh)]
                    if hh == 0:
                        wqb_waits += [("p2_mm", 8), ("p2_te", 24), ("p2_sq", 8)]
                    p.add("gpsimd", lambda e, hh=hh: e.dma_start(
                        out=wqb_sb[:],
                        in_=wqbT_d.rearrange("(c p) n -> p c n", p=128)[
                            :, :, hh * 8 * DQ:(hh + 1) * 8 * DQ]),
                        waits=wqb_waits, inc="d_w", inc_by=16)
                    wqb_ready = p.val("d_w")
                    for j in range(JT):
                        idx = 8 * hh + j
                        pi = idx % 2
                        rw = [("q_ro", idx - 1)] if idx >= 2 else [("p2_te", 24)]
                        p.add("sync", lambda e, j=j, hh=hh, pi=pi: e.dma_start(
                            out=cosq_sb[:, pi * 512:(pi + 1) * 512],
                            in_=cosq_d[j, :, hh * 512:(hh + 1) * 512]),
                            waits=rw, inc="d_misc", inc_by=16)
                        p.add("sync", lambda e, j=j, hh=hh, pi=pi: e.dma_start(
                            out=sinq_sb[:, pi * 512:(pi + 1) * 512],
                            in_=sinq_d[j, :, hh * 512:(hh + 1) * 512]),
                            inc="d_misc", inc_by=16)
                        rope_rdy = p.val("d_misc")

                        def qb_mms(e, j=j):
                            last = None
                            for ch in range(12):
                                for n in range(3):
                                    last = e.matmul(cq_ps[n][:],
                                                    cqn_fm[:, ch, j * 128:(j + 1) * 128],
                                                    wqb_sb[:, ch, n * 512:(n + 1) * 512],
                                                    start=(ch == 0), stop=(ch == 11),
                                                    skip_group_check=True)
                            return last
                        waits = [("d_w", wqb_ready), ("p2_te", 24)]
                        if idx >= 1:
                            waits.append(("q_ev", 3 * idx))
                        if idx == 0:
                            waits.append(("p2_ev", 24))
                        p.add("tensor", qb_mms, waits=waits, inc="q_mm")
                        for n in range(3):
                            ew = [("q_mm", idx + 1)]
                            if n == 0 and idx >= 2:
                                ew.append(("q_ro", idx - 1))
                            p.add("vector", lambda e, pi=pi, n=n: e.tensor_copy(
                                q_tm[pi][:, n * 512:(n + 1) * 512], cq_ps[n][:]),
                                waits=ew, inc="q_ev")

                        def pev(pi):
                            return q_tm[pi].ap().rearrange("p (h x) -> p h x", x=DQ)[:, :, DN:DQ]
                        t1v = t1q.ap().rearrange("p (h x) -> p h x", x=DR)
                        avv = avq.ap().rearrange("p (h x) -> p h x", x=DR)

                        def cosv(pi):
                            return cosq_sb[:, pi * 512:(pi + 1) * 512].rearrange(
                                "p (h x) -> p h x", x=DR)

                        def sinv(pi):
                            return sinq_sb[:, pi * 512:(pi + 1) * 512].rearrange(
                                "p (h x) -> p h x", x=DR)

                        p.add("vector", lambda e, pi=pi: e.tensor_copy(
                            t1v[:, :, 0:32], pev(pi)[:, :, 32:64]),
                            waits=[("q_ev", 3 * (idx + 1)), ("d_misc", rope_rdy)])
                        p.add("vector", lambda e, pi=pi: e.tensor_copy(
                            t1v[:, :, 32:64], pev(pi)[:, :, 0:32]))
                        p.add("vector", lambda e, pi=pi: e.tensor_tensor(
                            avv[:, :, :], pev(pi), cosv(pi), ALU.mult))
                        p.add("vector", lambda e, pi=pi: e.tensor_tensor(
                            t1v[:, :, :], t1v[:, :, :], sinv(pi), ALU.mult))
                        p.add("vector", lambda e, pi=pi: e.tensor_tensor(
                            pev(pi), avv[:, :, :], t1v[:, :, :], ALU.add))
                        cw = []
                        if idx >= 2:
                            cw.append(("q_tp", 8 * (idx - 1)))
                        p.add("vector", lambda e, pi=pi: e.tensor_copy(qbf[pi][:], q_tm[pi][:]),
                              waits=cw, inc="q_ro")
                        for h8 in range(8):
                            h = hh * 8 + h8
                            hb = h8 % 2
                            gidx = 8 * idx + h8

                            def qtp(e, pi=pi, h8=h8, hb=hb):
                                e.transpose(tp2_ps[hb][:, 0:128],
                                            qbf[pi][:, h8 * DQ:h8 * DQ + DN], ident_bf[:])
                                return e.transpose(tp2_ps[hb][0:DR, 128:256],
                                                   qbf[pi][:, h8 * DQ + DN:(h8 + 1) * DQ],
                                                   ident_bf[:])
                            tw = [("q_ro", idx + 1)]
                            if gidx >= 2:
                                tw.append(("q_te", gidx - 1))
                            p.add("tensor", qtp, waits=tw, inc="q_tp")
                            p.add("vector", lambda e, h=h, j=j, hb=hb: e.tensor_copy(
                                q_fm0[:, h, j, :], tp2_ps[hb][:, 0:128]),
                                waits=[("q_tp", gidx + 1)])
                            p.add("vector", lambda e, h=h, j=j, hb=hb: e.tensor_copy(
                                q_fm1[:, h, j, :], tp2_ps[hb][0:DR, 128:256]), inc="q_te")

                p.add("sync", waits=[("d_misc", p.val("d_misc"))])
                p.add("gpsimd", waits=[("d_w", p.val("d_w")), ("d_wb", p.val("d_wb")),
                                   ("d_wc", p.val("d_wc"))])
                if "qfm" in debug:
                    p.add("sync", lambda e: e.dma_start(out=dbg["dbg_q0"], in_=q_fm0[:]),
                          waits=[("q_te", p.val("q_te"))], inc="d_misc", inc_by=16)
                    p.add("sync", lambda e: e.dma_start(out=dbg["dbg_q1"], in_=q_fm1[:]),
                          inc="d_misc", inc_by=16)
                    p.add("sync", waits=[("d_misc", p.val("d_misc"))])
                _run_block(nc, "p2b", p, sems)

        if STOP_AFTER == "p2":
            qfm_ctx.close()
            kpe_ctx.close()
            ofm_ctx.close()
            return nc

        with ExitStack() as P3s:
            kbuf = [P3s.enter_context(nc.sbuf_tensor(f"kb{i}", [128, 8, 1024], BF16)) for i in range(2)]
            vbuf = [P3s.enter_context(nc.sbuf_tensor(f"vb{i}", [128, 8, 1024], BF16)) for i in range(2)]
            mask_sb = [P3s.enter_context(nc.sbuf_tensor(f"mk{i}", [128, 1024], BF16)) for i in range(4)]
            p_sb = [P3s.enter_context(nc.sbuf_tensor(f"p{i}", [128, 1024], BF16)) for i in range(2)]
            pt_sb = [P3s.enter_context(nc.sbuf_tensor(f"pt{i}", [128, 1024], BF16)) for i in range(2)]
            O_sb = P3s.enter_context(nc.sbuf_tensor("O_sb", [128, H, DN], F32))
            m_bufs = P3s.enter_context(nc.sbuf_tensor("m_bufs", [128, H, 2], F32))
            l_buf = P3s.enter_context(nc.sbuf_tensor("l_buf", [128, H], F32))
            c_buf = P3s.enter_context(nc.sbuf_tensor("c_buf", [128, H], F32))
            rs_buf = P3s.enter_context(nc.sbuf_tensor("rs_buf", [128, H], F32))
            nm_buf = P3s.enter_context(nc.sbuf_tensor("nm_buf", [128, H], F32))
            ofin = [P3s.enter_context(nc.sbuf_tensor(f"of{i}", [128, DN], BF16)) for i in range(2)]
            sc3 = P3s.enter_context(nc.sbuf_tensor("sc3", [128, 8], F32))

            s_ps = [P3s.enter_context(nc.psum_tensor(f"s{i}", [128, 1024], F32)) for i in range(2)]
            ptp_ps = [P3s.enter_context(nc.psum_tensor(f"ptp{i}", [128, 1024], BF16)) for i in range(2)]
            pv_ps = [P3s.enter_context(nc.psum_tensor(f"pv{i}", [128, 512], F32)) for i in range(2)]

            def c3(i):
                return sc3[:, i:i + 1]

            p = Prog(cnt)
            TILES = [(j, kt, hg) for j in range(JT) for kt in range(j + 1) for hg in range(2)]
            NTILE = len(TILES)
            NU = 8 * NTILE
            UNITS = []
            for i, (j, kt, hg) in enumerate(TILES):
                for h8 in range(8):
                    UNITS.append((i, j, kt, hg, hg * 8 + h8, h8))
            UEND = {}
            acc = 0
            for j in range(JT):
                acc += (j + 1) * 16
                UEND[j] = acc

            kbase = p.val("d_k")
            vbase = p.val("d_v")
            mbase = p.val("d_m")
            cc_need = p.val("cc")
            ccd_need = p.val("ccd")

            p.add("sync", lambda e: e.dma_start(
                out=kpe_fm[:].rearrange("d (c k) -> d c k", c=NCORES),
                in_=kpeg_v.rearrange("c d k -> d c k")),
                waits=[("cc", cc_need), ("ccd", ccd_need)], inc="d_misc", inc_by=16)
            kpe_rdy = p.val("d_misc")

            def emit_kv(i):
                j, kt, hg = TILES[i]
                b = i % 2
                ksem = "d_k" if b == 0 else "d_kb"
                vsem = "d_v" if b == 0 else "d_vb"
                kw, vw = [], []
                if i <= 1:
                    vw.append(("cc", cc_need))
                    vw.append(("ccd", ccd_need))
                    kw.append(("d_misc", kpe_rdy))
                if i >= 2:
                    kw.append(("t_s", 8 * (i - 1)))
                    vw.append(("t_pv", 8 * (i - 1)))
                p.add("sync", lambda e, kt=kt, hg=hg, b=b: e.dma_start(
                    out=kbuf[b][:],
                    in_=kg_v[kt, hg * 8:(hg + 1) * 8, :, :].rearrange("h p k -> p h k")),
                    waits=kw, inc=ksem, inc_by=16)
                p.add("gpsimd", lambda e, kt=kt, hg=hg, b=b: e.dma_start(
                    out=vbuf[b][:],
                    in_=vg_v[kt, hg * 8:(hg + 1) * 8, :, :].rearrange("h p n -> p h n")),
                    waits=vw, inc=vsem, inc_by=16)

            def emit_mask(j):
                w = []
                if j >= 4:
                    w.append(("v_mx", UEND[j - 4]))
                p.add("gpsimd", lambda e, j=j: e.dma_start(
                    out=mask_sb[j % 4][:], in_=mask_d[j, :, :]),
                    waits=w, inc="d_m%d" % (j % 4), inc_by=16)

            for jj in range(min(4, JT)):
                emit_mask(jj)
            emit_kv(0)
            emit_kv(1)

            def emit_A(u):
                i, j, kt, hg, h, h8 = UNITS[u]
                pi = u % 2
                b = i % 2

                def qk(e, pi=pi, h=h, h8=h8, b=b, kt=kt, j=j):
                    e.matmul(s_ps[pi][:, 0:512], q_fm0[:, h, j, :],
                             kbuf[b][:, h8, 0:512], start=True, stop=False,
                             skip_group_check=True)
                    e.matmul(s_ps[pi][:, 512:1024], q_fm0[:, h, j, :],
                             kbuf[b][:, h8, 512:1024], start=True, stop=False,
                             skip_group_check=True)
                    e.matmul(s_ps[pi][:, 0:512], q_fm1[:, h, j, :],
                             kpe_fm[:, kt * 1024:kt * 1024 + 512], start=False, stop=True,
                             skip_group_check=True)
                    return e.matmul(s_ps[pi][:, 512:1024], q_fm1[:, h, j, :],
                                    kpe_fm[:, kt * 1024 + 512:(kt + 1) * 1024],
                                    start=False, stop=True, skip_group_check=True)
                qw = [("d_k" if i % 2 == 0 else "d_kb", 16 * (i // 2 + 1))]
                if u >= 2:
                    qw.append(("s_ex", u - 1))
                p.add("tensor", qk, waits=qw, inc="t_s")

            def emit_G1(u):
                i, j, kt, hg, h, h8 = UNITS[u]
                p.add("vector", lambda e, h=h: e.scalar_tensor_tensor(
                    l_buf[:, h:h + 1], l_buf[:, h:h + 1], c_buf[:, h:h + 1],
                    rs_buf[:, h:h + 1], ALU.mult, ALU.add),
                    waits=[("s_ex", u + 1), ("t_pv", u + 1)])

            def emit_G2(u):
                i, j, kt, hg, h, h8 = UNITS[u]
                pi = u % 2
                p.add("vector", lambda e, h=h, pi=pi: e.scalar_tensor_tensor(
                    O_sb[:, h, :], O_sb[:, h, :], c_buf[:, h:h + 1],
                    pv_ps[pi][:, 0:DN], ALU.mult, ALU.add), inc="v_ou")

            def emit_D(uu):
                i2, j2, kt2, hg2, h2, h82 = UNITS[uu]
                pi2 = uu % 2

                def tps(e, pi=pi2):
                    last = None
                    for q in range(8):
                        last = e.transpose(ptp_ps[pi][:, q * 128:(q + 1) * 128],
                                           p_sb[pi][:, q * 128:(q + 1) * 128], ident_bf[:])
                    return last
                tw = [("s_ex", uu + 1)]
                if uu >= 2:
                    x = uu - 2
                    nv = (x + 1) // 2
                    if P3_EV_SPLIT:
                        if x % 2 == 1:
                            tw.append(("e_pv", nv))
                        else:
                            tw.append(("e_pt", x + 1 - nv))
                    else:
                        tw.append(("e_pt", uu - 1))
                if kt2 == 0 and hg2 == 0 and h82 == 0 and j2 >= 1:
                    tw.append(("v_f2", 16 * j2))
                p.add("tensor", tps, waits=tw, inc="t_pt")

            def emit_EF(uu):
                i2, j2, kt2, hg2, h2, h82 = UNITS[uu]
                pi2 = uu % 2
                if P3_EV_SPLIT and uu % 2 == 1:
                    p.add("vector", lambda e, pi=pi2: e.tensor_copy(
                        pt_sb[pi][:], ptp_ps[pi][:]),
                        waits=[("t_pt", uu + 1)], inc="e_pv")
                else:
                    p.add("scalar", lambda e, pi=pi2: e.copy(
                        out=pt_sb[pi][:], in_=ptp_ps[pi][:]),
                        waits=[("t_pt", uu + 1)], inc="e_pt")

                def pv(e, pi=pi2, h8=h82, b=i2 % 2):
                    last = None
                    for q in range(8):
                        last = e.matmul(pv_ps[pi][:, 0:DN],
                                        pt_sb[pi][:, q * 128:(q + 1) * 128],
                                        vbuf[b][:, h8, q * 128:(q + 1) * 128],
                                        start=(q == 0), stop=(q == 7),
                                        skip_group_check=True)
                    return last
                def _nv(x):
                    return (x + 1) // 2
                if P3_EV_SPLIT and uu % 2 == 1:
                    pw = [("e_pv", _nv(uu))]
                else:
                    pw = [("e_pt", (uu + 1 - _nv(uu)) if P3_EV_SPLIT else uu + 1)]
                pw.append(("d_v" if i2 % 2 == 0 else "d_vb", 16 * (i2 // 2 + 1)))
                if uu >= 2:
                    pw.append(("v_ou", uu - 1))
                p.add("tensor", pv, waits=pw, inc="t_pv")

            emit_A(0)
            u = 0
            g_done = -1
            d_done = -1
            ef_done = -1
            ev_done = -1

            def emit_Ev(uu):
                p.add("vector", lambda e, pi=uu % 2: e.tensor_copy(
                    pt_sb[pi][:, 0:384], ptp_ps[pi][:, 0:384]),
                    waits=[("t_pt", uu + 1)], inc="e_pv")
            for i, (j, kt, hg) in enumerate(TILES):
                first_of_j = (kt == 0 and hg == 0)
                if first_of_j:
                    iw = [("v_fi", 16 * j)] if j >= 1 else []
                    p.add("gpsimd", lambda e: e.memset(m_bufs[:], MINF), waits=iw)
                    p.add("gpsimd", lambda e: e.memset(l_buf[:], 0.0))
                    p.add("gpsimd", lambda e: e.memset(
                        O_sb[:].rearrange("p h d -> p (h d)"), 0.0), inc="a_in")
                if i + 2 < NTILE:
                    emit_kv(i + 2)
                masked = (kt == j)
                last_tile_of_j = (kt == j and hg == 1)
                for h8 in range(8):
                    h = hg * 8 + h8
                    pi = u % 2
                    ko = kt % 2
                    kn = (kt + 1) % 2

                    if u + 1 < NU:
                        emit_A(u + 1)

                    vw = [("t_s", u + 1)]
                    if first_of_j and h8 == 0:
                        vw.append(("a_in", j + 1))
                    if masked:
                        vw.append(("d_m%d" % (j % 4), 16 * (j // 4 + 1)))
                        p.add("vector", lambda e, pi=pi, j=j: e.scalar_tensor_tensor(
                            s_ps[pi][:], s_ps[pi][:], 1.0, mask_sb[j % 4][:],
                            ALU.mult, ALU.add), waits=vw)
                        vw = []
                    p.add("vector", lambda e, pi=pi: e.reduce_max(
                        c3(pi), s_ps[pi][:], axis=AX.X), waits=vw)
                    bubble_v(p)
                    p.add("vector", lambda e, h=h, ko=ko, kn=kn, pi=pi: e.tensor_tensor(
                        m_bufs[:, h, kn:kn + 1], m_bufs[:, h, ko:ko + 1], c3(pi), ALU.max))
                    bubble_v(p)
                    p.add("vector", lambda e, h=h, ko=ko, kn=kn, pi=pi: e.tensor_tensor(
                        c3(2 + pi), m_bufs[:, h, ko:ko + 1], m_bufs[:, h, kn:kn + 1],
                        ALU.subtract))
                    p.add("vector", lambda e, h=h, kn=kn: e.tensor_scalar(
                        nm_buf[:, h:h + 1], m_bufs[:, h, kn:kn + 1], -SCALE, None, ALU.mult))
                    bubble_v(p, inc="v_mx")
                    if u >= 3 and g_done < u - 3:
                        emit_G1(u - 3)
                        emit_G2(u - 3)
                        g_done = u - 3

                    if u >= 2 and ef_done < u - 2:
                        emit_EF(u - 2)
                        ef_done = u - 2

                    p.add("scalar", lambda e, pi=pi, h=h: e.activation(
                        out=c_buf[:, h:h + 1], in_=c3(2 + pi), func=AF.Exp, scale=SCALE),
                        waits=[("v_mx", u + 1)])
                    p.add("scalar", lambda e, pi=pi, h=h: e.activation(
                        out=p_sb[pi][:], in_=s_ps[pi][:], func=AF.Exp, scale=SCALE,
                        bias=nm_buf[:, h:h + 1], accum_out=rs_buf[:, h:h + 1]),
                        inc="s_ex")

                    if u >= 1 and d_done < u - 1:
                        emit_D(u - 1)
                        d_done = u - 1
                    u += 1

                if last_tile_of_j:
                    if d_done < u - 1:
                        emit_D(u - 1)
                        d_done = u - 1
                    while ef_done < u - 1:
                        emit_EF(ef_done + 1)
                        ef_done += 1
                    while g_done < u - 1:
                        emit_G1(g_done + 1)
                        emit_G2(g_done + 1)
                        g_done += 1
                    if j + 4 < JT:
                        emit_mask(j + 4)
                    for h in range(H + 1):
                        if h < H:
                            hb = h % 2
                            fw = [("v_ou", u)] if h == 0 else []
                            p.add("vector", lambda e, h=h, hb=hb: e.reciprocal(
                                c3(4 + hb), l_buf[:, h:h + 1]), waits=fw)
                        if h >= 1:
                            hp = h - 1
                            hb = hp % 2
                            p.add("vector", lambda e, hp=hp, hb=hb: e.tensor_scalar(
                                ofin[hb][:], O_sb[:, hp, :], c3(4 + hb), None, ALU.mult),
                                inc="v_fi")
                            tw = [("v_fi", 16 * j + hp + 1)]
                            if hp == 0:
                                tw.append(("e_pt", p.val("e_pt")))
                                if P3_EV_SPLIT:
                                    tw.append(("e_pv", p.val("e_pv")))
                            gfi = 16 * j + hp
                            if gfi >= 2:
                                tw.append(("v_f2", gfi - 1))
                            p.add("tensor", lambda e, hb=hb: e.transpose(
                                ptp_ps[hb][:, 0:128], ofin[hb][:], ident_bf[:]),
                                waits=tw, inc="t_fi")
                            p.add("vector", lambda e, hp=hp, j=j, hb=hb: e.tensor_copy(
                                o_fm[:, hp, j, :], ptp_ps[hb][:, 0:128]),
                                waits=[("t_fi", 16 * j + hp + 1)], inc="v_f2")

            p.add("sync", waits=[("d_k", p.val("d_k")), ("d_kb", p.val("d_kb"))])
            p.add("gpsimd", waits=[("d_v", p.val("d_v")), ("d_vb", p.val("d_vb")),
                                   ("d_m0", p.val("d_m0")), ("d_m1", p.val("d_m1")),
                                   ("d_m2", p.val("d_m2")), ("d_m3", p.val("d_m3"))])
            if "ofm" in debug:
                p.add("sync", lambda e: e.dma_start(out=dbg["dbg_ofm"], in_=o_fm[:]),
                      waits=[("v_f2", p.val("v_f2"))], inc="d_misc", inc_by=16)
                p.add("sync", waits=[("d_misc", p.val("d_misc"))])
            _run_block(nc, "p3", p, sems)
        if STOP_AFTER == "p3":
            qfm_ctx.close()
            kpe_ctx.close()
            ofm_ctx.close()
            return nc

        qfm_ctx.close()
        kpe_ctx.close()

        with ExitStack() as P4s:
            y_sb = P4s.enter_context(nc.sbuf_tensor("y_sb", [128, JT, D], F32))
            wbigQ = [P4s.enter_context(nc.sbuf_tensor(f"wbq{i}", [128, 16, 512], BF16))
                     for i in range(3)]
            yln_fm = P4s.enter_context(nc.sbuf_tensor("yln_fm", [128, 16, JT * 128], BF16))
            xt_sb = [P4s.enter_context(nc.sbuf_tensor(f"xt{i}", [128, 1024], F32)) for i in range(2)]
            ybf0 = P4s.enter_context(nc.sbuf_tensor("ybf", [128, D], BF16))
            ybf = [ybf0, ybf0]
            fs_sb = P4s.enter_context(nc.sbuf_tensor("fs_sb", [128, D], F32))
            fb_sb = P4s.enter_context(nc.sbuf_tensor("fb_sb", [128, D], F32))
            tmp0 = P4s.enter_context(nc.sbuf_tensor("tm0", [128, 512], F32))
            tmp_sb = [tmp0, tmp0]
            dump4 = ybf0
            sc4 = P4s.enter_context(nc.sbuf_tensor("sc4", [128, 16], F32))

            mm_ps = [P4s.enter_context(nc.psum_tensor(f"mm{i}", [128, 512], F32)) for i in range(2)]
            tp4_ps = [P4s.enter_context(nc.psum_tensor(f"tp4{i}", [128, 1024], BF16)) for i in range(2)]

            def c4(i):
                return sc4[:, i:i + 1]

            p = Prog(cnt)
            WSEM = ["d_w", "d_wb", "d_wc"]
            wq_snap = {}
            for k in range(4):
                qcol = (k // 2) * 1024 + (k % 2) * 512
                b = k % 3
                lw = [] if k < 3 else [("w_mm", 15)]
                p.add("gpsimd", lambda e, qcol=qcol, b=b: e.dma_start(
                    out=wbigQ[b][:],
                    in_=woT_d.rearrange("(c p) n -> p c n", p=128)[
                        :, :, qcol:qcol + 512]),
                    waits=lw, inc=WSEM[b], inc_by=16)
                wq_snap[k] = (WSEM[b], p.val(WSEM[b]))
            for half in range(2):
                for j in range(JT):
                    idx = 8 * half + j
                    p.add("sync", lambda e, half=half, j=j, idx=idx: e.dma_start(
                        out=xt_sb[idx % 2][:],
                        in_=xtok_d[j, :, half * 1024:(half + 1) * 1024]),
                        waits=[("w_ev", 2 * (idx - 1))] if idx >= 2 else [],
                        inc="d_xs" if idx % 2 == 0 else "d_xsb", inc_by=16)
                    for n in range(2):
                        gi = 2 * idx + n
                        nb = gi % 2

                        kq = 2 * half + n

                        def womm(e, j=j, b=kq % 3, nb=nb):
                            last = None
                            for ch in range(16):
                                last = e.matmul(mm_ps[nb][:],
                                                o_fm[:, ch, j, :],
                                                wbigQ[b][:, ch, :],
                                                start=(ch == 0), stop=(ch == 15))
                            return last
                        mw = [wq_snap[kq]]
                        if gi >= 2:
                            mw.append(("w_ev", gi - 1))
                        p.add("tensor", womm, waits=mw, inc="w_mm")
                        p.add("vector", lambda e, j=j, half=half, n=n, nb=nb, idx=idx:
                              e.scalar_tensor_tensor(
                                  y_sb[:, j, half * 1024 + n * 512:half * 1024 + (n + 1) * 512],
                                  mm_ps[nb][:], 1.0,
                                  xt_sb[idx % 2][:, n * 512:(n + 1) * 512], ALU.mult, ALU.add),
                              waits=[("w_mm", gi + 1),
                                     ("d_xs" if idx % 2 == 0 else "d_xsb",
                                      16 * (idx // 2 + 1))],
                              inc="w_ev")

            def emit_ln(j, phase, ev_sem, ev_need, out_dma):
                pi = j % 2
                lw = [(ev_sem, ev_need)]
                sqw = list(lw)
                if phase == 0 and j >= 1:
                    sqw.append(("y_tp", 4 * j))
                p.add("vector", lambda e, j=j, pi=pi: e.reduce_sum(
                    c4(0 + pi), y_sb[:, j, :], axis=AX.X), waits=lw)
                p.add("scalar", lambda e, j=j, pi=pi: e.activation(
                    out=dump4[:], in_=y_sb[:, j, :], func=AF.Square, accum_out=c4(2 + pi)),
                    waits=sqw, inc="l_sm")
                sm_need = p.val("l_sm")
                bubble_v(p)
                p.add("vector", lambda e, pi=pi: e.tensor_scalar(
                    c4(4 + pi), c4(0 + pi), -1.0 / D, None, ALU.mult))
                p.add("vector", lambda e, pi=pi: e.tensor_scalar(
                    c4(6 + pi), c4(2 + pi), 1.0 / D, EPS_LN, ALU.mult, ALU.add),
                    waits=[("l_sm", sm_need)])
                p.add("vector", lambda e, pi=pi: e.tensor_tensor(
                    c4(8 + pi), c4(4 + pi), c4(4 + pi), ALU.mult))
                bubble_v(p)
                p.add("vector", lambda e, pi=pi: e.scalar_tensor_tensor(
                    c4(10 + pi), c4(8 + pi), -1.0, c4(6 + pi), ALU.mult, ALU.add))
                bubble_v(p)
                p.add("vector", lambda e, pi=pi: e.reciprocal(c4(12 + pi), c4(10 + pi)))
                bubble_v(p, inc="y_ln")
                rc_need = p.val("y_ln")
                p.add("scalar", lambda e, pi=pi: e.activation(
                    out=c4(14 + pi), in_=c4(12 + pi), func=AF.Sqrt),
                    waits=[("y_ln", rc_need)])
                bubble_s(p, AF.Sqrt, inc="l_sq")
                sq_need = p.val("l_sq")
                p.add("vector", lambda e, j=j, pi=pi: e.tensor_scalar(
                    y_sb[:, j, :], y_sb[:, j, :], c4(4 + pi), c4(14 + pi), ALU.add, ALU.mult),
                    waits=[("l_sq", sq_need)],
                    inc="z_ln" if out_dma else None)
                if out_dma:
                    p.add("sync", lambda e, j=j: e.dma_start(
                        out=out_d[j, :, :], in_=y_sb[:, j, :]),
                        waits=[("z_ln", p.val("z_ln"))], inc="d_out", inc_by=16)

            for j in range(JT):
                pi = j % 2
                emit_ln(j, 0, "w_ev", 16 + 2 * (j + 1), False)
                cw = [("y_ln", p.val("y_ln"))]
                if j >= 1:
                    cw.append(("y_tp", 4 * j))
                p.add("vector", lambda e, j=j, pi=pi: e.tensor_copy(ybf[pi][:], y_sb[:, j, :]),
                      waits=cw, inc="y_ln")
                ybf_need = p.val("y_ln")
                for r in range(4):
                    ridx = 4 * j + r
                    rb = ridx % 2

                    def ytp(e, pi=pi, r=r, rb=rb):
                        last = None
                        for q in range(4):
                            last = e.transpose(tp4_ps[rb][:, q * 128:(q + 1) * 128],
                                               ybf[pi][:, (4 * r + q) * 128:(4 * r + q + 1) * 128],
                                               ident_bf[:])
                        return last
                    tw = [("y_ln", ybf_need)]
                    if ridx >= 2:
                        tw.append(("y_te", ridx - 1))
                    p.add("tensor", ytp, waits=tw, inc="y_tp")
                    p.add("vector", lambda e, j=j, r=r, rb=rb: e.tensor_copy(
                        yln_fm[:, 4 * r:4 * r + 4, j * 128:(j + 1) * 128],
                        tp4_ps[rb][:, 0:512].rearrange("p (c x) -> p c x", x=128)),
                        waits=[("y_tp", ridx + 1)], inc="y_te")

            p.add("sync", lambda e: e.dma_start(out=fs_sb[:], in_=fs_d[:, :]),
                  inc="d_misc", inc_by=16)
            p.add("sync", lambda e: e.dma_start(out=fb_sb[:], in_=fb_d[:, :]),
                  inc="d_misc", inc_by=16)
            fsrdy = p.val("d_misc")
            for j in range(JT):
                p.add("vector", lambda e, j=j: e.tensor_tensor(
                    y_sb[:, j, :], y_sb[:, j, :], fb_sb[:], ALU.add),
                    waits=[("y_te", 4 * JT), ("d_misc", fsrdy)] if j == 0 else ())
            FQW = {4: ("w_mm", 16), 5: ("w_mm", 31), 6: ("w_mm", 32), 7: ("f_mm", 15)}
            for k in range(4, 8):
                qcol = ((k - 4) // 2) * 1024 + (k % 2) * 512
                b = k % 3
                p.add("gpsimd", lambda e, qcol=qcol, b=b: e.dma_start(
                    out=wbigQ[b][:],
                    in_=wfT_d.rearrange("(c p) n -> p c n", p=128)[
                        :, :, qcol:qcol + 512]),
                    waits=[FQW[k]], inc=WSEM[b], inc_by=16)
                wq_snap[k] = (WSEM[b], p.val(WSEM[b]))
            for half in range(2):
                for j in range(JT):
                    idx = 8 * half + j
                    for n in range(2):
                        gi = 2 * idx + n
                        nb = gi % 2
                        off = half * 1024 + n * 512

                        kq = 4 + 2 * half + n

                        def ffmm(e, j=j, b=kq % 3, nb=nb):
                            last = None
                            for ch in range(16):
                                last = e.matmul(mm_ps[nb][:],
                                                yln_fm[:, ch, j * 128:(j + 1) * 128],
                                                wbigQ[b][:, ch, :],
                                                start=(ch == 0), stop=(ch == 15))
                            return last
                        mw = [wq_snap[kq], ("y_te", 4 * (j + 1))]
                        if gi >= 2:
                            mw.append(("f_ev", gi - 1))
                        else:
                            mw.append(("w_ev", 32))
                        p.add("tensor", ffmm, waits=mw, inc="f_mm")
                        p.add("vector", lambda e, nb=nb, off=off: e.tensor_tensor(
                            tmp_sb[nb][:], mm_ps[nb][:], fs_sb[:, off:off + 512], ALU.mult),
                            waits=[("f_mm", gi + 1)])
                        p.add("vector", lambda e, j=j, nb=nb, off=off: e.tensor_tensor(
                            y_sb[:, j, off:off + 512], y_sb[:, j, off:off + 512],
                            tmp_sb[nb][:], ALU.add), inc="f_ev")
            for j in range(JT):
                emit_ln(j, 1, "f_ev", 16 + 2 * (j + 1), True)

            if "yln" in debug:
                p.add("sync", lambda e: e.dma_start(out=dbg["dbg_yln"], in_=y_sb[:]),
                      waits=[("y_te", 32)], inc="d_misc", inc_by=16)
            p.add("gpsimd", waits=[("d_w", p.val("d_w")), ("d_wb", p.val("d_wb")),
                                   ("d_wc", p.val("d_wc"))])
            p.add("sync", waits=[("d_out", p.val("d_out")), ("d_misc", p.val("d_misc")),
                                 ("d_xs", p.val("d_xs")), ("d_xsb", p.val("d_xsb"))])
            _run_block(nc, "p45", p, sems)

        ofm_ctx.close()
    return nc



def _host_prepare(inputs):
    x = np.ascontiguousarray(np.asarray(inputs["L0_attn_hidden_states"], dtype=np.float32)[0])
    w_qa = np.asarray(inputs["L0_attn_w_qa"], dtype=np.float32)
    q_a_ln_w = np.asarray(inputs["L0_attn_q_a_ln_w"], dtype=np.float32)
    w_qb = np.asarray(inputs["L0_attn_w_qb"], dtype=np.float32)
    w_kva = np.asarray(inputs["L0_attn_w_kva"], dtype=np.float32)
    kv_a_ln_w = np.asarray(inputs["L0_attn_kv_a_ln_w"], dtype=np.float32)
    w_kvb = np.asarray(inputs["L0_attn_w_kvb"], dtype=np.float32)
    w_o = np.asarray(inputs["L0_attn_w_o"], dtype=np.float32)
    f_w = np.asarray(inputs["L0_ffn_weight"], dtype=np.float32)
    f_s = np.asarray(inputs["L0_ffn_scale_weight"], dtype=np.float32)
    f_b = np.asarray(inputs["L0_ffn_bias"], dtype=np.float32)

    C = np.ascontiguousarray
    shared = {
        "wqaT": C(w_qa.T),
        "wqbT": C(w_qb.T),
        "wkvaT": C(w_kva.T),
        "wkbkT": C(w_kvb.reshape(H, DN + DV, KVLR)[:, :DN].reshape(H * DN, KVLR).T),
        "wkbvT": C(w_kvb.reshape(H, DN + DV, KVLR)[:, DN:].reshape(H * DV, KVLR).T),
        "woT": C(w_o.T),
        "wfT": C(f_w.T),
        "gq": C(np.broadcast_to(q_a_ln_w, (128, QLR))),
        "gkv": C(np.broadcast_to(kv_a_ln_w, (128, KVLR))),
        "fs": C(np.broadcast_to(f_s, (128, D))),
        "fb": C(np.broadcast_to(f_b, (128, D))),
        "ident": np.eye(128, dtype=np.float32),
    }
    pos = np.arange(S, dtype=np.float32)
    inv_freq = 1.0 / (10000.0 ** (np.arange(0, DR, 2, dtype=np.float32) / DR))
    freqs = np.outer(pos, inv_freq)
    emb = np.concatenate([freqs, freqs], axis=1)
    cos = np.cos(emb).astype(np.float32)
    sin = np.sin(emb).astype(np.float32)
    sinS = np.concatenate([-sin[:, :DR // 2], sin[:, DR // 2:]], axis=1)
    cosk_t = cos.reshape(NT, 128, DR)
    sink_t = sinS.reshape(NT, 128, DR)

    xt = x.reshape(NT, 128, D)
    in_maps = []
    for c in range(NCORES):
        tiles = [8 * j + c for j in range(JT)]
        xc = xt[tiles]
        cosq = np.tile(cos.reshape(NT, 128, 1, DR)[tiles], (1, 1, H, 1))
        sinq = np.tile(sinS.reshape(NT, 128, 1, DR)[tiles], (1, 1, H, 1))
        mask = np.zeros((JT, 128, 1024), dtype=np.float32)
        for j in range(JT):
            t = 8 * j + c
            qpos = t * 128 + np.arange(128)[:, None]
            kpos = j * 1024 + np.arange(1024)[None, :]
            mask[j] = np.where(kpos <= qpos, 0.0, NEGBIG)
        m = dict(shared)
        m["xtok"] = C(xc)
        m["xT_own"] = C(xc.reshape(JT * 128, D).T)
        m["xT_kv"] = C(x[1024 * c:1024 * (c + 1)].T)
        m["cosk"] = C(cosk_t[8 * c:8 * c + 8])
        m["sink"] = C(sink_t[8 * c:8 * c + 8])
        m["cosq"] = C(cosq.reshape(JT, 128, H * DR))
        m["sinq"] = C(sinq.reshape(JT, 128, H * DR))
        m["mask"] = mask
        in_maps.append(m)
    return in_maps


def kernel(**inputs):
    global LAST_RESULTS
    in_maps = _host_prepare(inputs)
    nc = build_program(debug=DEBUG)
    res = run_bass_kernel_spmd(nc, in_maps, core_ids=list(range(NCORES)))
    if DEBUG:
        LAST_RESULTS = res
    y = np.empty((NT, 128, D), dtype=np.float32)
    for c in range(NCORES):
        o = np.asarray(res.results[c]["out"])
        for j in range(JT):
            y[8 * j + c] = o[j]
    return y.reshape(1, S, D)
